# version 16
# baseline (speedup 1.0000x reference)
import math
import numpy as np
import concourse.bass as bass
import concourse.mybir as mybir
from concourse.bass_utils import run_bass_kernel_spmd
from contextlib import ExitStack

F32 = mybir.dt.float32
BF16 = mybir.dt.bfloat16
I32 = mybir.dt.int32
AF = mybir.ActivationFunctionType
ALU = mybir.AluOpType
AX = mybir.AxisListType

T = 2048
D = 2048
FF = 8192
NT = 16
SQ = math.sqrt(128.0)
ENGS = ("tensor", "vector", "scalar", "gpsimd", "sync")


def slope(h):
    return 2.0 ** (-(h // 4 + 1) / 2.0)


class Prog:
    def __init__(self):
        self.items = {e: [] for e in ENGS}
        self.cnt = {}
        self.res = {}

    def now(self, sem):
        return self.cnt.get(sem, 0)

    def w(self, sem):
        return (sem, self.now(sem))

    def use(self, name):
        return self.res.get(name, ("gdn", 0))

    def done(self, name, sem):
        self.res[name] = (sem, self.now(sem))

    def op(self, eng, fn, waits=(), inc=None, amt=1):
        self.items[eng].append((list(waits), fn, inc, amt))
        if inc is not None:
            self.cnt[inc] = self.cnt.get(inc, 0) + amt

    def vbub(self, waits=(), inc=None):
        self.op("vector",
                lambda e: e.tensor_copy(self.bubA[:], self.bubB[:]),
                waits, inc=inc)

    def dma(self, qeng, sem, out, in_, waits=()):
        self.op(qeng, lambda e, o=out, i=in_: e.dma_start(out=o, in_=i),
                waits, inc=sem, amt=16)

    def fence(self):
        snap = [(s, v) for s, v in self.cnt.items()]
        for e in ENGS:
            self.items[e].append((snap, None, None, 0))

    def emit(self, block, sems):
        def run(eng_obj, name):
            emitted = {}
            for waits, fn, inc, amt in self.items[name]:
                for s, v in waits:
                    if v > 0 and emitted.get(s, -1) < v:
                        eng_obj.wait_ge(sems[s], v)
                        emitted[s] = v
                if fn is not None:
                    instr = fn(eng_obj)
                    if inc is not None:
                        instr.then_inc(sems[inc], amt)

        @block.tensor
        def _(e):
            run(e, "tensor")

        @block.vector
        def _(e):
            run(e, "vector")

        @block.scalar
        def _(e):
            run(e, "scalar")

        @block.gpsimd
        def _(e):
            run(e, "gpsimd")

        @block.sync
        def _(e):
            run(e, "sync")


def build():
    nc = bass.Bass("TRN2", target_bir_lowering=False, debug=False)

    x_h = nc.dram_tensor("x", [T, D], F32, kind="ExternalInput").ap()
    wq_h = nc.dram_tensor("wq", [D, D], F32, kind="ExternalInput").ap()
    wk_h = nc.dram_tensor("wk", [D, 512], F32, kind="ExternalInput").ap()
    wv_h = nc.dram_tensor("wv", [D, 512], F32, kind="ExternalInput").ap()
    wo_h = nc.dram_tensor("wo", [D, D], F32, kind="ExternalInput").ap()
    wup_h = nc.dram_tensor("wup", [FF, D], F32, kind="ExternalInput").ap()
    wdn_h = nc.dram_tensor("wdn", [D, FF], F32, kind="ExternalInput").ap()
    nw_h = nc.dram_tensor("nw", [1, D], F32, kind="ExternalInput").ap()
    gam_h = nc.dram_tensor("gam", [1, D], F32, kind="ExternalInput").ap()
    out_h = nc.dram_tensor("out", [T, D], F32, kind="ExternalOutput").ap()

    xT_h = nc.dram_tensor("xTs", [D, T], F32, kind="Internal").ap()
    yT_h = nc.dram_tensor("yTs", [D, T], F32, kind="Internal").ap()
    x1_h = nc.dram_tensor("x1s", [T, D], F32, kind="Internal").ap()
    wupT_h = nc.dram_tensor("wupT", [D, FF], BF16, kind="Internal").ap()
    wdnT_h = nc.dram_tensor("wdnT", [FF, D], BF16, kind="Internal").ap()
    vecs_h = nc.dram_tensor("vecs", [1, T], F32, kind="Internal").ap()
    scal_h = nc.dram_tensor("scal", [1, 2], F32, kind="Internal").ap()

    xT3 = xT_h.rearrange("(c p) t -> p c t", p=128)
    wupT3 = wupT_h.rearrange("(c p) h -> p c h", p=128)
    wdnT3 = wdnT_h.rearrange("(b p) c -> p b c", p=128)
    vecs2 = vecs_h.rearrange("a (b c) -> (a b) c", b=16)

    p = Prog()
    SEMS = ("qld", "gld", "tdn", "vdn", "adn", "gdn", "sld")

    ctx = ExitStack()
    with ctx:
        sb = lambda name, shape, dt: ctx.enter_context(nc.sbuf_tensor(name, shape, dt))
        sems = {s: ctx.enter_context(nc.semaphore(s)) for s in SEMS}

        ident = sb("ident", [128, 128], F32)
        identb = sb("identb", [128, 128], BF16)
        zft = sb("zft", [128, 128], F32)
        zbt = sb("zbt", [128, 128], BF16)
        p.bubA = sb("bubA", [128, 1], F32)
        p.bubB = sb("bubB", [128, 1], F32)
        p.gbubA = sb("gbubA", [128, 1], F32)
        p.gbubB = sb("gbubB", [128, 1], F32)
        p.sbubA = sb("sbubA", [128, 1], F32)
        p.sbubB = sb("sbubB", [128, 1], F32)
        ones = sb("ones", [128, 1], F32)
        epsln = sb("epsln", [128, 1], F32)
        epsrms = sb("epsrms", [128, 1], F32)
        STNAMES = ["m0", "m1", "m2", "m3", "bias0", "bias1", "bias2",
                   "bias3", "sig0", "sig1", "sig2", "sig3",
                   "r0", "r1", "r2", "r3",
                   "sp0", "sp1", "sp2", "sp3", "sums", "nm", "ssq", "t1",
                   "m2t", "var", "sd", "rstd", "gmax", "gclp", "rg", "c",
                   "sx", "tstat", "waccu", "waccd",
                   "sq0", "sq1", "sq2", "sq3"]
        stt = {n: sb("st_" + n, [128, 1], F32) for n in STNAMES}
        mean_bc = [sb(f"mean_bc{i}", [128, 1], F32) for i in range(2)]
        sval_bc = [sb(f"sval_bc{i}", [128, 1], F32) for i in range(2)]
        thrp_bc = [sb(f"thrp_bc{i}", [128, 1], F32) for i in range(2)]
        thrn_bc = [sb(f"thrn_bc{i}", [128, 1], F32) for i in range(2)]
        mean_sb = sb("mean_sb", [1, 2], F32)
        sxs_stash = sb("sxs_stash", [128, 16], F32)
        banks = [ctx.enter_context(nc.psum_tensor(f"pb{i}", [128, 512], F32))
                 for i in range(6)]

        PARN = ("m", "bias", "sig", "r")

        def sc(name, par=0):
            key = f"{name}{par}" if name in PARN else name
            return stt[key][:]

        def gop(fn, waits=()):
            p.op("gpsimd", fn, waits, inc="gdn")

        gop(lambda e: e.memset(ones[:], 1.0))
        gop(lambda e: e.memset(p.bubB[:], 0.0))
        gop(lambda e: e.memset(p.gbubB[:], 0.0))
        gop(lambda e: e.memset(p.sbubB[:], 0.0))
        gop(lambda e: e.memset(epsln[:], 1e-5))
        gop(lambda e: e.memset(epsrms[:], 1e-6))
        gop(lambda e: e.memset(zft[:], 0.0))
        gop(lambda e: e.memset(zbt[:], 0.0))
        for t_ in (ident, identb):
            gop(lambda e, t2=t_: e.memset(t2[:], 1.0))
            gop(lambda e, t2=t_: e.affine_select(
                t2[:], t2[:], [[-1, 128]], ALU.is_ge, 0.0,
                base=0, channel_multiplier=1))
            gop(lambda e, t2=t_: e.affine_select(
                t2[:], t2[:], [[1, 128]], ALU.is_ge, 0.0,
                base=0, channel_multiplier=-1))

        bf_ctx = ExitStack()
        with bf_ctx:
            wstage = [bf_ctx.enter_context(
                nc.sbuf_tensor(f"wst{i}", [128, 2048], F32)) for i in range(4)]
            sxrow = bf_ctx.enter_context(
                nc.sbuf_tensor("sxrow", [16, 128], F32))

            actx = ExitStack()
            with actx:
                asb = lambda name, shape, dt: actx.enter_context(
                    nc.sbuf_tensor(name, shape, dt))
                Ccol = asb("Ccol", [128, T], F32)
                maskb = asb("maskb", [128, 896], F32)
                K_all = asb("K_all", [128, 4 * T], F32)
                V_all = asb("V_all", [128, 16 * 512], F32)
                Qpair = asb("Qpair", [128, 2 * T], F32)
                Ssb = [asb(f"Ssb{i}", [128, T], F32) for i in range(3)]
                Ssb.append(wstage[3])
                Psb = asb("Psb", [128, T], F32)
                dtmp = asb("dtmp", [128, 512], F32)
                PTb2 = [asb(f"PTb{i}", [128, 16 * 128], F32)
                        for i in range(2)]
                yh = [asb(f"yh{i}", [128, T], F32) for i in range(2)]
                wqst = [asb(f"wqst{i}", [128, 256], F32) for i in range(2)]
                estage = [asb(f"est{i}", [128, 1024], F32) for i in range(2)]
                wkvst = [estage[i][:, 0:512] for i in range(2)]
                ptp = actx.enter_context(
                    nc.psum_tensor("ptp", [128, 1024], F32))
                ptp3 = ptp[:].rearrange("p (a b) -> p a b", a=2)
                PTb3 = [t[:].rearrange("p (k ab) -> p k ab", ab=128)
                        for t in PTb2]

                gop(lambda e: e.memset(K_all[:], 0.0))
                gop(lambda e: e.memset(V_all[:], 0.0))
                gop(lambda e: e.iota(Ccol[:], [[1, T]], base=0,
                                     channel_multiplier=0,
                                     allow_small_or_imprecise_dtypes=True))
                gop(lambda e: e.memset(maskb[:], 0.0))
                gop(lambda e: e.affine_select(
                    maskb[:], maskb[:], [[-1, 896]], ALU.is_ge, -1e33,
                    base=384, channel_multiplier=1))
                A_DONE = p.w("gdn")

                for ti in range(NT):
                    slot = wstage[ti % 2]
                    p.dma("gpsimd", "gld", slot[:],
                          x_h[ti * 128:(ti + 1) * 128, :],
                          waits=[p.use(f"wst{ti % 2}"), A_DONE])
                    ld = p.w("gld")
                    half = PTb2[ti % 2]
                    for dc in range(16):
                        bk = banks[dc % 2]
                        p.op("tensor",
                             lambda e, s=slot, d=dc, b=bk: e.transpose(
                                 b[:, 0:128], s[:, d * 128:(d + 1) * 128],
                                 ident[:]),
                             waits=[ld, p.use(f"bank{dc % 2}"), A_DONE],
                             inc="tdn")
                        p.op("vector",
                             lambda e, b=bk, hf=half, d=dc: e.tensor_copy(
                                 hf[:][:, d * 128:(d + 1) * 128], b[:, 0:128]),
                             waits=[p.w("tdn"), p.use(f"ptbh{ti % 2}")],
                             inc="vdn")
                        p.done(f"bank{dc % 2}", "vdn")
                    p.done(f"wst{ti % 2}", "tdn")
                    p.dma("gpsimd", "gld", xT3[:, :, ti * 128:(ti + 1) * 128],
                          half[:].rearrange("p (c t) -> p c t", c=16),
                          waits=[p.w("vdn")])
                    p.done(f"ptbh{ti % 2}", "gld")
                p.fence()

                for pi in range(4):
                    kv = pi
                    for dc in range(16):
                        si = dc % 3
                        xslot = wstage[si]
                        p.dma("sync", "qld", xslot[:],
                              xT_h[dc * 128:(dc + 1) * 128, :],
                              waits=[p.use(f"wst{si}")])
                        wslot = wkvst[dc % 2]
                        p.dma("scalar", "sld", wslot,
                              wv_h[dc * 128:(dc + 1) * 128, :],
                              waits=[p.use(f"est{dc % 2}")])
                        qslot = wqst[dc % 2]
                        p.dma("scalar", "sld", qslot[:, 0:128],
                              wk_h[dc * 128:(dc + 1) * 128,
                                   kv * 128:(kv + 1) * 128],
                              waits=[p.use(f"wqst{dc % 2}")])
                        ld = p.w("qld")
                        lds = p.w("sld")
                        first, last = dc == 0, dc == 15
                        for kt in range(4):
                            bw = [p.use(f"bank{kt}")] if first else []
                            p.op("tensor",
                                 lambda e, q=qslot, xs=xslot, k=kt, f=first,
                                 l=last: e.matmul(
                                     banks[k][:], q[:, 0:128],
                                     xs[:, k * 512:(k + 1) * 512],
                                     start=f, stop=l),
                                 waits=[ld, lds] + bw)
                        for vt in range(4):
                            ti = pi * 4 + vt
                            bname = (f"bank{4 + vt}" if vt < 2
                                     else f"ptph{vt - 2}")
                            dst = (banks[4][:] if vt == 0 else banks[5][:]
                                   if vt == 1 else
                                   ptp[:, (vt - 2) * 512:(vt - 1) * 512])
                            bw = [p.use(bname)] if first else []
                            p.op("tensor",
                                 lambda e, xs=xslot, w2=wslot, t=ti, d=dst,
                                 f=first, l=last: e.matmul(
                                     d, xs[:, t * 128:(t + 1) * 128],
                                     w2, start=f, stop=l),
                                 waits=[ld, lds] + bw,
                                 inc="tdn" if vt == 3 else None)
                        p.done(f"wst{si}", "tdn")
                        p.done(f"est{dc % 2}", "tdn")
                        p.done(f"wqst{dc % 2}", "tdn")
                    mmd = p.w("tdn")
                    for kt in range(4):
                        p.op("vector",
                             lambda e, k=kt, kvv=kv: e.tensor_copy(
                                 K_all[:, kvv * T + k * 512:
                                       kvv * T + (k + 1) * 512],
                                 banks[k][:]),
                             waits=[mmd], inc="vdn")
                        p.done(f"bank{kt}", "vdn")
                    for vt in range(4):
                        ti = pi * 4 + vt
                        bname = f"bank{4 + vt}" if vt < 2 else f"ptph{vt - 2}"
                        src = (banks[4][:] if vt == 0 else banks[5][:]
                               if vt == 1 else
                               ptp[:, (vt - 2) * 512:(vt - 1) * 512])
                        p.op("vector",
                             lambda e, s=src, t=ti: e.tensor_copy(
                                 V_all[:, t * 512:(t + 1) * 512], s),
                             waits=[mmd], inc="vdn")
                        p.done(bname, "vdn")
                p.fence()

                def emit_e1_chunk(ch):
                    si = ch % 2
                    slot = estage[si]
                    if ch < 128:
                        src = wup_h[(ch // 2) * 128:(ch // 2 + 1) * 128,
                                    (ch % 2) * 1024:(ch % 2 + 1) * 1024]
                        acc = sc("waccu")
                    else:
                        c2 = ch - 128
                        cc, qq = c2 // 8, c2 % 8
                        src = wdn_h[cc * 128:(cc + 1) * 128,
                                    qq * 1024:(qq + 1) * 1024]
                        acc = sc("waccd")
                    p.dma("gpsimd", "gld", slot[:], src,
                          waits=[p.use(f"est{si}")])
                    ld = p.w("gld")
                    p.op("vector",
                         lambda e, s=slot: e.reduce_sum(
                             sc("tstat"), s[:], axis=AX.X,
                             apply_absolute_value=True),
                         waits=[ld], inc="vdn")
                    p.done(f"est{si}", "vdn")
                    p.vbub()
                    p.vbub()
                    if ch % 128 == 0:
                        p.op("vector",
                             lambda e, a=acc: e.tensor_copy(a, sc("tstat")),
                             waits=[], inc="vdn")
                    else:
                        p.op("vector",
                             lambda e, a=acc: e.scalar_tensor_tensor(
                                 a, sc("tstat"), 1.0, a, ALU.mult, ALU.add),
                             waits=[], inc="vdn")

                for pr in range(8):
                    for dc in range(16):
                        si = dc % 3
                        xslot = wstage[si]
                        p.dma("sync", "qld", xslot[:],
                              xT_h[dc * 128:(dc + 1) * 128, :],
                              waits=[p.use(f"wst{si}")])
                        qslot = wqst[dc % 2]
                        p.dma("scalar", "sld", qslot[:],
                              wq_h[dc * 128:(dc + 1) * 128,
                                   pr * 256:(pr + 1) * 256],
                              waits=[p.use(f"wqst{dc % 2}")])
                        ld = p.w("qld")
                        lds = p.w("sld")
                        first, last = dc == 0, dc == 15
                        for gi in range(8):
                            hh, tg = gi // 4, gi % 4
                            bname = f"bank{gi}" if gi < 6 else f"ptph{gi - 6}"
                            dst = (banks[gi][:] if gi < 6
                                   else ptp[:, (gi - 6) * 512:(gi - 5) * 512])
                            bw = [p.use(bname)] if first else []
                            p.op("tensor",
                                 lambda e, q=qslot, xs=xslot, h2=hh, t=tg,
                                 d=dst, f=first, l=last: e.matmul(
                                     d, q[:, h2 * 128:(h2 + 1) * 128],
                                     xs[:, t * 512:(t + 1) * 512],
                                     start=f, stop=l),
                                 waits=[ld, lds] + bw,
                                 inc="tdn" if gi == 7 else None)
                        p.done(f"wst{si}", "tdn")
                        p.done(f"wqst{dc % 2}", "tdn")
                    qmm = p.w("tdn")
                    for gi in range(8):
                        hh, tg = gi // 4, gi % 4
                        scl = 1.0 / (SQ * slope(2 * pr + hh))
                        bname = f"bank{gi}" if gi < 6 else f"ptph{gi - 6}"
                        src = (banks[gi][:] if gi < 6
                               else ptp[:, (gi - 6) * 512:(gi - 5) * 512])
                        p.op("vector",
                             lambda e, s=src, h2=hh, t=tg, sc2=scl:
                             e.tensor_scalar_mul(
                                 Qpair[:, h2 * T + t * 512:
                                       h2 * T + (t + 1) * 512], s, sc2),
                             waits=[qmm, p.use("Qpair")], inc="vdn")
                        p.done(bname, "vdn")
                    QEV = p.w("vdn")

                    kv = (2 * pr) // 4
                    s_h = slope(2 * pr)
                    chain_done = {}

                    def emit_scores(hh, qi, par):
                        nkt = qi // 4 + 1
                        kpad = 512 * nkt
                        dkt = qi // 4
                        mvar = qi % 4
                        smm = []
                        for kt in range(nkt):
                            p.op("tensor",
                                 lambda e, k=kt, q=qi, h2=hh, kvv=kv:
                                 e.matmul(
                                     banks[k][:],
                                     Qpair[:, h2 * T + q * 128:
                                           h2 * T + (q + 1) * 128],
                                     K_all[:, kvv * T + k * 512:
                                           kvv * T + (k + 1) * 512],
                                     start=True, stop=True),
                                 waits=[QEV, p.use(f"bank{kt}")],
                                 inc="tdn")
                            smm.append(p.w("tdn"))
                        for kt in range(nkt):
                            if kt == dkt:
                                p.op("vector",
                                     lambda e, k=kt: e.tensor_add(
                                         dtmp[:], banks[k][:],
                                         Ccol[:, k * 512:(k + 1) * 512]),
                                     waits=[smm[kt], p.use(f"Ssb{par}")],
                                     inc="vdn")
                                p.op("vector",
                                     lambda e, k=kt, pa=par, v=mvar:
                                     e.tensor_add(
                                         Ssb[pa][:, k * 512:(k + 1) * 512],
                                         dtmp[:],
                                         maskb[:, 384 - 128 * v:
                                               896 - 128 * v]),
                                     waits=[], inc="vdn")
                            else:
                                p.op("vector",
                                     lambda e, k=kt, pa=par: e.tensor_add(
                                         Ssb[pa][:, k * 512:(k + 1) * 512],
                                         banks[k][:],
                                         Ccol[:, k * 512:(k + 1) * 512]),
                                     waits=[smm[kt], p.use(f"Ssb{par}")],
                                     inc="vdn")
                            p.done(f"bank{kt}", "vdn")
                        p.op("vector",
                             lambda e, pa=par, kp=kpad: e.reduce_max(
                                 sc("m", pa), Ssb[pa][:, 0:kp], axis=AX.X),
                             waits=[], inc="vdn")
                        p.vbub()
                        p.vbub()
                        p.op("vector",
                             lambda e, pa=par, s2=s_h: e.tensor_scalar_mul(
                                 sc("bias", pa), sc("m", pa), -s2),
                             waits=[], inc="vdn")
                        p.vbub()
                        p.vbub()
                        p.vbub(inc="vdn")
                        vd = p.w("vdn")
                        p.op("scalar",
                             lambda e, pa=par, kp=kpad, s2=s_h: e.activation(
                                 Psb[:, 0:kp], Ssb[pa][:, 0:kp], AF.Exp,
                                 bias=sc("bias", pa), scale=s2,
                                 accum_out=sc("sig", pa)),
                             waits=[vd], inc=None)
                        p.op("scalar",
                             lambda e: e.copy(p.sbubA[:], p.sbubB[:]),
                             waits=[], inc="adn")
                        ad = p.w("adn")
                        p.vbub(waits=[ad])
                        p.vbub()
                        p.vbub()
                        p.op("vector",
                             lambda e, pa=par: e.reciprocal(
                                 sc("r", pa), sc("sig", pa)),
                             waits=[], inc="vdn")
                        p.vbub()
                        p.vbub()
                        p.op("vector",
                             lambda e, pa=par, kp=kpad: e.tensor_scalar(
                                 Ssb[pa][:, 0:kp], Psb[:, 0:kp],
                                 sc("r", pa), None, ALU.mult),
                             waits=[], inc="vdn")
                        chain_done[(hh, qi)] = p.w("vdn")

                    def emit_pt_pv(hh, qi, par):
                        nkt = qi // 4 + 1
                        kpad = 512 * nkt
                        nkb = kpad // 128
                        pnd = chain_done[(hh, qi)]
                        REGN = ("ptph0", "ptph1", "bank5")
                        for kb in range(nkb):
                            reg = REGN[kb % 3]
                            dst = (ptp[:, 0:128] if kb % 3 == 0 else
                                   ptp[:, 512:640] if kb % 3 == 1 else
                                   banks[5][:, 0:128])
                            p.op("tensor",
                                 lambda e, pa=par, k=kb, d2=dst:
                                 e.transpose(
                                     d2, Ssb[pa][:, k * 128:(k + 1) * 128],
                                     ident[:]),
                                 waits=[pnd, p.use(reg),
                                        p.use(f"PTb{hh}")],
                                 inc="tdn")
                            td = p.w("tdn")
                            p.op("scalar",
                                 lambda e, k=kb, h2=hh, d2=dst:
                                 e.copy(
                                     PTb2[h2][:, k * 128:(k + 1) * 128], d2),
                                 waits=[td], inc="adn")
                            p.done(reg, "adn")
                        p.done(f"Ssb{par}", "tdn")
                        ptev = p.w("adn")
                        for kc in range(nkb):
                            p.op("tensor",
                                 lambda e, k=kc, h2=hh, kvv=kv,
                                 f=(kc == 0), l=(kc == nkb - 1):
                                 e.matmul(
                                     banks[4][:, 0:128],
                                     V_all[:, k * 512 + kvv * 128:
                                           k * 512 + (kvv + 1) * 128],
                                     PTb2[h2][:, k * 128:(k + 1) * 128],
                                     start=f, stop=l),
                                 waits=[ptev, p.use("bank4")],
                                 inc="tdn" if kc == nkb - 1 else None)
                        p.done(f"PTb{hh}", "tdn")
                        pvd = p.w("tdn")
                        p.op("scalar",
                             lambda e, q=qi, h2=hh: e.copy(
                                 yh[h2][:, q * 128:(q + 1) * 128],
                                 banks[4][:, 0:128]),
                             waits=[pvd, p.use(f"yh{hh}")],
                             inc="adn")
                        p.done("bank4", "adn")

                    for idx in range(32):
                        hh, qi = idx % 2, idx // 2
                        emit_scores(hh, qi, idx % 4)
                        if idx >= 3:
                            j = idx - 3
                            emit_pt_pv(j % 2, j // 2, j % 4)
                        emit_e1_chunk(pr * 32 + idx)
                    for j in range(29, 32):
                        emit_pt_pv(j % 2, j // 2, j % 4)
                    for hh in range(2):
                        h = 2 * pr + hh
                        p.dma("gpsimd", "gld", yT_h[h * 128:(h + 1) * 128, :],
                              yh[hh][:], waits=[p.w("vdn"), p.w("adn")])
                        p.done(f"yh{hh}", "gld")
                    p.done("Qpair", "tdn")
                p.fence()

                for wi in range(2):
                    acc = sc("waccu") if wi == 0 else sc("waccd")
                    p.vbub()
                    p.vbub()
                    p.vbub(inc="vdn")
                    vd = p.w("vdn")
                    p.op("tensor",
                         lambda e, a=acc: e.matmul(banks[0][0:1, 0:1], a,
                                                   ones[:], start=True,
                                                   stop=True),
                         waits=[vd], inc="tdn")
                    td = p.w("tdn")
                    p.op("vector",
                         lambda e, w2=wi: e.tensor_scalar_mul(
                             mean_sb[0:1, w2:w2 + 1], banks[0][0:1, 0:1],
                             1.0 / (FF * D)),
                         waits=[td], inc="vdn")
                p.dma("gpsimd", "gld", scal_h[:], mean_sb[:],
                      waits=[p.w("vdn")])
                for w2 in range(2):
                    p.dma("sync", "qld", mean_bc[w2][:],
                          scal_h[:, w2:w2 + 1].partition_broadcast(128),
                          waits=[p.w("gld")])
                mbcl = p.w("qld")
                for w2 in range(2):
                    p.op("vector",
                         lambda e, i=w2: e.tensor_scalar_max(
                             sval_bc[i][:], mean_bc[i][:], 1e-8),
                         waits=[mbcl], inc="vdn")
                    p.vbub()
                    p.vbub()
                    p.op("vector",
                         lambda e, i=w2: e.tensor_scalar_mul(
                             thrp_bc[i][:], sval_bc[i][:], 0.5),
                         waits=[], inc="vdn")
                    p.vbub()
                    p.vbub()
                    p.op("vector",
                         lambda e, i=w2: e.tensor_scalar_mul(
                             thrn_bc[i][:], thrp_bc[i][:], -1.0),
                         waits=[], inc="vdn")
                p.vbub()
                p.vbub()
                p.vbub(inc="vdn")
                THRLD = p.w("vdn")
                p.fence()

            xqT = ctx.enter_context(
                nc.sbuf_tensor("xqT", [128, 16 * 2048], BF16, side="right"))

            e2f_ctx = ExitStack()
            with e2f_ctx:
                esb = lambda name, shape, dt: e2f_ctx.enter_context(
                    nc.sbuf_tensor(name, shape, dt))
                tern1 = esb("tern1", [128, 1024], F32)
                tern2 = esb("tern2", [128, 1024], F32)
                ternf = esb("ternf", [128, 1024], F32)
                tstage = [esb(f"tstage{i}", [128, 1024], BF16) for i in range(2)]
                estg = [esb(f"e2st{i}", [128, 1024], F32) for i in range(2)]
                xres = [esb(f"xres{i}", [128, 2048], F32) for i in range(2)]
                f1 = esb("f1", [128, 2048], F32)
                f2 = esb("f2", [128, 2048], F32)
                f3 = esb("f3", [128, 2048], F32)
                f4 = esb("f4", [128, 2048], F32)
                f5 = [esb(f"f5_{i}", [128, 2048], F32) for i in range(2)]
                i32t = esb("i32t", [128, 2048], I32)
                nw_rep = esb("nw_rep", [128, 2048], F32)
                fps = e2f_ctx.enter_context(
                    nc.psum_tensor("fps", [128, 1024], F32))

                def fbank(g):
                    return (banks[g][:] if g < 6
                            else fps[:, (g - 6) * 512:(g - 5) * 512])

                p.dma("sync", "qld", nw_rep[:],
                      nw_h.partition_broadcast(128), waits=[])
                NWLD = p.w("qld")

                def emit_e2_unit(u):
                    si = u % 2
                    slot = estg[si]
                    ts = tstage[si]
                    if u < 128:
                        rb, hh2 = u // 2, u % 2
                        src2 = wup_h[rb * 128:(rb + 1) * 128,
                                     hh2 * 1024:(hh2 + 1) * 1024]
                        wi2 = 0
                        dst3 = wupT3[:, hh2 * 8:(hh2 + 1) * 8,
                                     rb * 128:(rb + 1) * 128]
                    else:
                        c2 = u - 128
                        cc, qq = c2 // 8, c2 % 8
                        src2 = wdn_h[cc * 128:(cc + 1) * 128,
                                     qq * 1024:(qq + 1) * 1024]
                        wi2 = 1
                        dst3 = wdnT3[:, qq * 8:(qq + 1) * 8,
                                     cc * 128:(cc + 1) * 128]
                    p.dma("gpsimd", "gld", slot[:], src2,
                          waits=[p.use(f"e2st{si}")])
                    ldq = p.w("gld")
                    p.op("gpsimd",
                         lambda e, s=slot, i=wi2: e.tensor_scalar(
                             tern1[:], s[:], thrp_bc[i][:], None, ALU.is_ge),
                         waits=[ldq, THRLD], inc="gdn")
                    p.op("gpsimd",
                         lambda e, s=slot, i=wi2: e.tensor_scalar(
                             tern2[:], s[:], thrn_bc[i][:], None, ALU.is_le),
                         waits=[], inc="gdn")
                    p.done(f"e2st{si}", "gdn")
                    p.op("gpsimd",
                         lambda e: e.tensor_sub(ternf[:], tern1[:], tern2[:]),
                         waits=[p.use("ternf")], inc="gdn")
                    t0 = p.w("gdn")
                    for b in range(8):
                        bi = b % 2
                        p.op("tensor",
                             lambda e, b2=b, i=bi: e.transpose(
                                 banks[i][:, 0:128],
                                 ternf[:, b2 * 128:(b2 + 1) * 128],
                                 ident[:]),
                             waits=[t0, p.use(f"bank{bi}")], inc="tdn")
                        p.op("vector",
                             lambda e, b2=b, i=bi, t=ts: e.tensor_copy(
                                 t[:, b2 * 128:(b2 + 1) * 128],
                                 banks[i][:, 0:128]),
                             waits=[p.w("tdn")] +
                             ([p.use(f"tstage{si}")] if b == 0 else []),
                             inc="vdn")
                        p.done(f"bank{bi}", "vdn")
                    p.done("ternf", "tdn")
                    p.dma("gpsimd", "gld", dst3,
                          ts[:].rearrange("p (b c) -> p b c", b=8),
                          waits=[p.w("vdn")])
                    p.done(f"tstage{si}", "gld")

                xq_pend = []
                e2_next = [0]

                def emit_xq_transpose(ti, QED):
                    for dc in range(16):
                        bk = banks[dc % 2]
                        p.op("tensor",
                             lambda e, d=dc, b=bk, t=ti: e.transpose(
                                 b[:, 0:128],
                                 f5[t % 2][:, d * 128:(d + 1) * 128],
                                 ident[:]),
                             waits=[QED, p.use(f"bank{dc % 2}")],
                             inc="tdn")
                        p.op("vector",
                             lambda e, d=dc, b=bk, t=ti: e.tensor_scalar(
                                 xqT[:, d * 2048 + t * 128:
                                     d * 2048 + (t + 1) * 128],
                                 b[:, 0:128], 127.0, -127.0,
                                 ALU.min, ALU.max),
                             waits=[p.w("tdn")], inc="vdn")
                        p.done(f"bank{dc % 2}", "vdn")
                    p.done(f"f5_{ti % 2}", "tdn")

                for tp in range(8):
                    for t2 in range(2):
                        ti = tp * 2 + t2
                        p.dma("gpsimd", "gld", xres[t2][:],
                              x_h[ti * 128:(ti + 1) * 128, :],
                              waits=[p.use(f"xres{t2}")])
                    XR = p.w("gld")
                    for fc in range(16):
                        yi = 2 * (fc % 2)
                        yst = wstage[yi]
                        wst = wstage[yi + 1]
                        p.dma("sync", "qld", yst[:],
                              yT_h[fc * 128:(fc + 1) * 128, :],
                              waits=[p.use(f"wst{yi}")])
                        p.dma("scalar", "sld", wst[:],
                              wo_h[fc * 128:(fc + 1) * 128, :],
                              waits=[p.use(f"wst{yi + 1}")])
                        ld = p.w("qld")
                        lds = p.w("sld")
                        first, last = fc == 0, fc == 15
                        for gi in range(8):
                            t2, ct = gi // 4, gi % 4
                            bname = f"bank{gi}" if gi < 6 else f"fps{gi - 6}"
                            bw = [p.use(bname)] if first else []
                            p.op("tensor",
                                 lambda e, y=yst, w2=wst, tt=t2, c=ct,
                                 g=gi, f=first, l=last, t0=tp:
                                 e.matmul(
                                     fbank(g),
                                     y[:, (t0 * 2 + tt) * 128:
                                       (t0 * 2 + tt + 1) * 128],
                                     w2[:, c * 512:(c + 1) * 512],
                                     start=f, stop=l),
                                 waits=[ld, lds] + bw,
                                 inc="tdn" if gi == 7 else None)
                        p.done(f"wst{yi}", "tdn")
                        p.done(f"wst{yi + 1}", "tdn")
                    OMM = p.w("tdn")
                    for t2 in range(2):
                        for ct in range(4):
                            gi = t2 * 4 + ct
                            bname = f"bank{gi}" if gi < 6 else f"fps{gi - 6}"
                            p.op("vector",
                                 lambda e, g=gi, c=ct, xr=xres[t2], t4=t2:
                                 e.scalar_tensor_tensor(
                                     f1[:, c * 512:(c + 1) * 512]
                                     if t4 == 0 else
                                     f2[:, c * 512:(c + 1) * 512],
                                     fbank(g), 1.0,
                                     xr[:, c * 512:(c + 1) * 512],
                                     ALU.mult, ALU.add,
                                     accum_out=sc("sp%d" % c) if t4 == 0
                                     else sc("sq%d" % c)),
                                 waits=[OMM, XR], inc="vdn")
                            p.done(bname, "vdn")
                        p.done(f"xres{t2}", "vdn")
                    for ti_, QED_ in xq_pend:
                        emit_xq_transpose(ti_, QED_)
                    xq_pend.clear()
                    for _ in range(32):
                        if e2_next[0] < 256:
                            emit_e2_unit(e2_next[0])
                            e2_next[0] += 1
                    for t2 in range(2):
                        ti = tp * 2 + t2
                        fin = f1 if t2 == 0 else f2
                        ac = "sp" if t2 == 0 else "sq"
                        p.vbub()
                        p.vbub()
                        p.op("vector", lambda e, a=ac: e.tensor_add(
                            sc("sums"), sc(a + "0"), sc(a + "1")),
                            waits=[], inc="vdn")
                        p.op("vector", lambda e, a=ac: e.tensor_add(
                            sc("t1"), sc(a + "2"), sc(a + "3")),
                            waits=[], inc="vdn")
                        p.vbub()
                        p.vbub()
                        p.op("vector", lambda e: e.tensor_add(
                            sc("sums"), sc("sums"), sc("t1")),
                            waits=[], inc="vdn")
                        p.vbub()
                        p.vbub()
                        p.op("vector", lambda e: e.tensor_scalar_mul(
                            sc("nm"), sc("sums"), -1.0 / D),
                            waits=[], inc="vdn")
                        p.op("vector", lambda e, fi=fin: e.scalar_tensor_tensor(
                            f3[:], fi[:], 1.0, fi[:], ALU.mult, ALU.mult,
                            accum_out=sc("ssq")),
                            waits=[], inc="vdn")
                        p.vbub()
                        p.vbub()
                        p.op("vector", lambda e: e.tensor_scalar_mul(
                            sc("t1"), sc("ssq"), 1.0 / D),
                            waits=[], inc="vdn")
                        p.op("vector", lambda e: e.tensor_mul(
                            sc("m2t"), sc("nm"), sc("nm")),
                            waits=[], inc="vdn")
                        p.vbub()
                        p.vbub()
                        p.op("vector", lambda e: e.tensor_sub(
                            sc("var"), sc("t1"), sc("m2t")),
                            waits=[], inc="vdn")
                        p.vbub()
                        p.vbub()
                        p.vbub(inc="vdn")
                        vd = p.w("vdn")
                        p.op("scalar", lambda e: e.activation(
                            sc("sd"), sc("var"), AF.Sqrt, bias=epsln[:]),
                            waits=[vd], inc=None)
                        p.op("scalar",
                             lambda e: e.copy(p.sbubA[:], p.sbubB[:]),
                             waits=[], inc="adn")
                        ad = p.w("adn")
                        p.vbub(waits=[ad])
                        p.vbub()
                        p.vbub()
                        p.op("vector", lambda e: e.reciprocal(
                            sc("rstd"), sc("sd")), waits=[], inc="vdn")
                        p.op("vector", lambda e, fi=fin: e.tensor_scalar(
                            f3[:], fi[:], sc("nm"), None, ALU.add),
                            waits=[], inc="vdn")
                        p.op("vector", lambda e: e.tensor_scalar(
                            f4[:], f3[:], sc("rstd"), None, ALU.mult),
                            waits=[p.use("x1sp")], inc="vdn")
                        p.dma("gpsimd", "gld",
                              x1_h[ti * 128:(ti + 1) * 128, :], f4[:],
                              waits=[p.w("vdn")])
                        p.done("x1sp", "gld")
                        p.op("vector", lambda e: e.scalar_tensor_tensor(
                            f3[:], f4[:], 1.0, f4[:], ALU.mult, ALU.mult,
                            accum_out=sc("ssq")),
                            waits=[], inc="vdn")
                        p.vbub()
                        p.vbub()
                        p.op("vector", lambda e: e.tensor_scalar_mul(
                            sc("t1"), sc("ssq"), 1.0 / D),
                            waits=[], inc="vdn")
                        p.vbub()
                        p.vbub()
                        p.vbub(inc="vdn")
                        vd = p.w("vdn")
                        p.op("scalar", lambda e: e.activation(
                            sc("sd"), sc("t1"), AF.Sqrt, bias=epsrms[:]),
                            waits=[vd], inc=None)
                        p.op("scalar",
                             lambda e: e.copy(p.sbubA[:], p.sbubB[:]),
                             waits=[], inc="adn")
                        ad = p.w("adn")
                        p.vbub(waits=[ad])
                        p.vbub()
                        p.vbub()
                        p.op("vector", lambda e: e.reciprocal(
                            sc("rstd"), sc("sd")), waits=[], inc="vdn")
                        p.vbub()
                        p.vbub()
                        p.op("vector", lambda e: e.tensor_scalar(
                            f3[:], f4[:], sc("rstd"), None, ALU.mult),
                            waits=[], inc="vdn")
                        p.op("vector", lambda e, fi=fin: e.tensor_mul(
                            fi[:], f3[:], nw_rep[:]),
                            waits=[NWLD], inc="vdn")
                        p.op("vector", lambda e, fi=fin: e.reduce_max(
                            sc("gmax"), fi[:], axis=AX.X,
                            apply_absolute_value=True),
                            waits=[], inc="vdn")
                        p.vbub()
                        p.vbub()
                        p.op("vector", lambda e: e.tensor_scalar_max(
                            sc("gclp"), sc("gmax"), 1e-8),
                            waits=[], inc="vdn")
                        p.vbub()
                        p.vbub()
                        p.op("vector", lambda e: e.reciprocal(
                            sc("rg"), sc("gclp")), waits=[], inc="vdn")
                        p.vbub()
                        p.vbub()
                        p.op("vector", lambda e: e.tensor_scalar_mul(
                            sc("c"), sc("rg"), 127.0), waits=[], inc="vdn")
                        p.op("vector", lambda e: e.tensor_scalar_mul(
                            sc("sx"), sc("gclp"), 1.0 / 127.0),
                            waits=[], inc="vdn")
                        p.vbub()
                        p.vbub()
                        p.op("vector", lambda e, t=ti: e.tensor_scalar(
                            sxs_stash[:, t:t + 1], sc("sx"),
                            sval_bc[0][:], None, ALU.mult),
                            waits=[THRLD], inc="vdn")
                        p.op("vector", lambda e, fi=fin: e.tensor_scalar(
                            f3[:], fi[:], sc("c"), None, ALU.mult),
                            waits=[], inc="vdn")
                        p.op("vector", lambda e: e.tensor_copy(
                            i32t[:], f3[:]), waits=[], inc="vdn")
                        fp = f5[ti % 2]
                        p.op("vector", lambda e, fp=fp: e.tensor_copy(
                            fp[:], i32t[:]), waits=[p.use(f"f5_{ti % 2}")],
                            inc="vdn")
                        p.op("vector", lambda e, fp=fp: e.tensor_sub(
                            f4[:], f3[:], fp[:]), waits=[p.use("x1sp")],
                            inc="vdn")
                        p.op("vector", lambda e: e.tensor_single_scalar(
                            f3[:], f4[:], 0.5, ALU.is_ge),
                            waits=[], inc="vdn")
                        p.op("vector", lambda e, fp=fp: e.tensor_add(
                            fp[:], fp[:], f3[:]), waits=[], inc="vdn")
                        p.op("vector", lambda e: e.tensor_single_scalar(
                            f3[:], f4[:], -0.5, ALU.is_lt),
                            waits=[], inc="vdn")
                        p.op("vector", lambda e, fp=fp: e.tensor_sub(
                            fp[:], fp[:], f3[:]), waits=[], inc="vdn")
                        xq_pend.append((ti, p.w("vdn")))
                for ti, QED in xq_pend:
                    emit_xq_transpose(ti, QED)
                for u in range(e2_next[0], 256):
                    emit_e2_unit(u)
                sxv = p.w("vdn")
                p.op("tensor", lambda e: e.transpose(
                    banks[0][0:16, 0:128], sxs_stash[:], ident[:]),
                    waits=[sxv, p.use("bank0")], inc="tdn")
                p.op("vector", lambda e: e.tensor_copy(
                    sxrow[:], banks[0][0:16, 0:128]),
                    waits=[p.w("tdn")], inc="vdn")
                p.dma("gpsimd", "gld", vecs2[:], sxrow[:],
                      waits=[p.w("vdn")])
                p.fence()

        gctx = ExitStack()
        with gctx:
            gsb = lambda name, shape, dt: gctx.enter_context(
                nc.sbuf_tensor(name, shape, dt))
            sxg = [gsb(f"sxg{i}", [128, 512], F32) for i in range(2)]
            gam_rep = gsb("gam_rep", [128, 2048], F32)
            hfhi = gsb("hfhi", [128, 8 * 512], BF16)
            hflo = gsb("hflo", [128, 8 * 512], BF16)
            oacc2 = [gsb(f"oacc{i}", [128, 4 * 2048], F32) for i in range(2)]
            wub = [gsb(f"wub{i}", [128, 8 * 128], BF16) for i in range(2)]
            wdnb = [gsb(f"wdnb{i}", [128, 2048], BF16) for i in range(2)]
            g1 = gsb("g1", [128, 2048], F32)
            g2 = gsb("g2", [128, 2048], F32)
            sfu = gsb("sfu", [128, 512], F32)
            hbuf = gsb("hbuf", [128, 512], F32)
            x1r = [gsb(f"x1r{i}", [128, 2048], F32) for i in range(2)]
            gps = gctx.enter_context(nc.psum_tensor("gps", [128, 1024], F32))

            def gbank(g):
                return (banks[g][:] if g < 6
                        else gps[:, (g - 6) * 512:(g - 5) * 512])

            p.dma("sync", "qld", gam_rep[:],
                  gam_h.partition_broadcast(128), waits=[])
            GAMLD = p.w("qld")

            def emit_epilogue(tg2):
                oc = oacc2[tg2 % 2]
                for tt in range(4):
                    ti = tg2 * 4 + tt
                    p.dma("gpsimd", "gld", x1r[tt % 2][:],
                          x1_h[ti * 128:(ti + 1) * 128, :],
                          waits=[p.use(f"x1r{tt % 2}")])
                    X1L = p.w("gld")
                    p.op("vector",
                         lambda e, t3=tt, o=oc: e.tensor_mul(
                             g1[:], o[:, t3 * 2048:(t3 + 1) * 2048],
                             gam_rep[:]),
                         waits=[GAMLD, p.use("g1")], inc="vdn")
                    p.op("vector", lambda e: e.tensor_scalar(
                        g2[:], g1[:], sval_bc[1][:], None, ALU.mult),
                        waits=[], inc="vdn")
                    p.op("vector",
                         lambda e, t3=tt: e.scalar_tensor_tensor(
                             g1[:], g2[:], 1.0, x1r[t3 % 2][:],
                             ALU.mult, ALU.add, accum_out=sc("sums")),
                         waits=[X1L], inc="vdn")
                    p.done(f"x1r{tt % 2}", "vdn")
                    p.op("vector", lambda e: e.scalar_tensor_tensor(
                        g2[:], g1[:], 1.0, g1[:], ALU.mult, ALU.mult,
                        accum_out=sc("ssq")),
                        waits=[], inc="vdn")
                    p.vbub()
                    p.vbub()
                    p.op("vector", lambda e: e.tensor_scalar_mul(
                        sc("nm"), sc("sums"), -1.0 / D),
                        waits=[], inc="vdn")
                    p.op("vector", lambda e: e.tensor_scalar_mul(
                        sc("t1"), sc("ssq"), 1.0 / D),
                        waits=[], inc="vdn")
                    p.vbub()
                    p.vbub()
                    p.op("vector", lambda e: e.tensor_mul(
                        sc("m2t"), sc("nm"), sc("nm")),
                        waits=[], inc="vdn")
                    p.vbub()
                    p.vbub()
                    p.op("vector", lambda e: e.tensor_sub(
                        sc("var"), sc("t1"), sc("m2t")),
                        waits=[], inc="vdn")
                    p.vbub()
                    p.vbub()
                    p.vbub(inc="vdn")
                    vd = p.w("vdn")
                    p.op("scalar", lambda e: e.activation(
                        sc("sd"), sc("var"), AF.Sqrt, bias=epsln[:]),
                        waits=[vd], inc=None)
                    p.op("scalar",
                         lambda e: e.copy(p.sbubA[:], p.sbubB[:]),
                         waits=[], inc="adn")
                    ad = p.w("adn")
                    p.vbub(waits=[ad])
                    p.vbub()
                    p.vbub()
                    p.op("vector", lambda e: e.reciprocal(
                        sc("rstd"), sc("sd")), waits=[], inc="vdn")
                    p.vbub()
                    p.vbub()
                    p.op("vector", lambda e: e.tensor_scalar(
                        g2[:], g1[:], sc("nm"), None, ALU.add),
                        waits=[], inc="vdn")
                    p.op("vector", lambda e: e.tensor_scalar(
                        g1[:], g2[:], sc("rstd"), None, ALU.mult),
                        waits=[], inc="vdn")
                    p.dma("gpsimd", "gld",
                          out_h[ti * 128:(ti + 1) * 128, :], g1[:],
                          waits=[p.w("vdn")])
                    p.done("g1", "gld")

            for tg in range(4):
                p.dma("sync", "qld", sxg[tg % 2][:],
                      vecs_h[:, tg * 512:(tg + 1) * 512]
                      .partition_broadcast(128),
                      waits=[p.use(f"sxg{tg % 2}")])
                SXLD = p.w("qld")
                for e8 in range(8):
                    for s in range(8):
                        hsub = e8 * 8 + s
                        wls = []
                        for j in range(2):
                            p.dma("sync", "qld", wub[j][:],
                                  wupT3[:, j * 8:(j + 1) * 8,
                                        hsub * 128:(hsub + 1) * 128],
                                  waits=[p.use(f"wub{j}")])
                            wls.append(p.w("qld"))
                        ubn = f"bank{4 + s % 2}"
                        upb = banks[4 + s % 2]
                        for dc in range(16):
                            bw = [p.use(ubn)] if dc == 0 else []
                            lw = [wls[dc // 8]] if dc % 8 == 0 else []
                            p.op("tensor",
                                 lambda e, w2=wub[dc // 8], d=dc, b=upb,
                                 f=(dc == 0), t0=tg:
                                 e.matmul(
                                     b[:],
                                     w2[:, (d % 8) * 128:(d % 8 + 1) * 128],
                                     xqT[:, d * 2048 + t0 * 512:
                                         d * 2048 + (t0 + 1) * 512],
                                     start=f, stop=False),
                                 waits=lw + bw,
                                 inc="tdn" if dc == 7 else None)
                            if dc == 7:
                                p.done("wub0", "tdn")
                        p.op("tensor",
                             lambda e, b=upb: e.matmul(
                                 b[:], zft[:], gam_rep[:, 0:512],
                                 start=False, stop=True),
                             waits=[], inc="tdn")
                        p.done("wub1", "tdn")
                        um = p.w("tdn")
                        p.op("vector",
                             lambda e, b=upb, t0=tg: e.tensor_mul(
                                 sfu[:], b[:], sxg[t0 % 2][:]),
                             waits=[um, SXLD], inc="vdn")
                        p.done(ubn, "vdn")
                        vd = p.w("vdn")
                        p.op("scalar",
                             lambda e: e.activation(hbuf[:], sfu[:], AF.Silu),
                             waits=[vd], inc=None)
                        p.op("scalar",
                             lambda e: e.copy(p.sbubA[:], p.sbubB[:]),
                             waits=[], inc="adn")
                        ad = p.w("adn")
                        p.op("vector",
                             lambda e, s2=s: e.tensor_copy(
                                 hfhi[:, s2 * 512:(s2 + 1) * 512], hbuf[:]),
                             waits=[ad, p.use("hfeat")], inc="vdn")
                        p.op("vector",
                             lambda e, s2=s: e.scalar_tensor_tensor(
                                 hflo[:, s2 * 512:(s2 + 1) * 512], hbuf[:],
                                 1.0, hfhi[:, s2 * 512:(s2 + 1) * 512],
                                 ALU.mult, ALU.subtract),
                             waits=[], inc="vdn")
                    HFD = p.w("vdn")
                    for half in range(2):
                        for s in range(8):
                            hsub = e8 * 8 + s
                            p.dma("sync", "qld", wdnb[s % 2][:],
                                  wdnT_h[hsub * 128:(hsub + 1) * 128, :],
                                  waits=[p.use(f"wdnb{s % 2}")])
                            wl = p.w("qld")
                            for t2 in range(2):
                                tt = half * 2 + t2
                                for ct in range(4):
                                    gi = t2 * 4 + ct
                                    bname = (f"bank{gi}" if gi < 6
                                             else f"gps{gi - 6}")
                                    for hilo in range(2):
                                        hf = hfhi if hilo == 0 else hflo
                                        st = s == 0 and hilo == 0
                                        bw = [p.use(bname)] if st else []
                                        p.op("tensor",
                                             lambda e, h3=hf, s2=s, t3=tt,
                                             c=ct, g=gi, w2=wdnb[s % 2],
                                             f=st: e.matmul(
                                                 gbank(g),
                                                 h3[:, s2 * 512 + t3 * 128:
                                                    s2 * 512 + (t3 + 1) * 128],
                                                 w2[:, c * 512:(c + 1) * 512],
                                                 start=f, stop=False),
                                             waits=[HFD, wl] + bw,
                                             inc="tdn"
                                             if (hilo == 1 and ct == 3 and
                                                 t2 == 1) else None)
                            p.done(f"wdnb{s % 2}", "tdn")
                        for gi in range(8):
                            p.op("tensor",
                                 lambda e, g=gi: e.matmul(
                                     gbank(g), zbt[:], xqT[:, 0:512],
                                     start=False, stop=True),
                                 waits=[], inc="tdn" if gi == 7 else None)
                        dmm = p.w("tdn")
                        for t2 in range(2):
                            tt = half * 2 + t2
                            for ct in range(4):
                                gi = t2 * 4 + ct
                                bname = (f"bank{gi}" if gi < 6
                                         else f"gps{gi - 6}")
                                col = tt * 2048 + ct * 512
                                oc = oacc2[tg % 2]
                                if e8 == 0:
                                    p.op("vector",
                                         lambda e, g=gi, c2=col, o=oc:
                                         e.tensor_copy(
                                             o[:, c2:c2 + 512], gbank(g)),
                                         waits=[dmm], inc="vdn")
                                else:
                                    p.op("vector",
                                         lambda e, g=gi, c2=col, o=oc:
                                         e.scalar_tensor_tensor(
                                             o[:, c2:c2 + 512], gbank(g),
                                             1.0, o[:, c2:c2 + 512],
                                             ALU.mult, ALU.add),
                                         waits=[dmm], inc="vdn")
                                p.done(bname, "vdn")
                    p.done("hfeat", "tdn")
                p.done(f"sxg{tg % 2}", "vdn")
                if tg >= 1:
                    emit_epilogue(tg - 1)
            emit_epilogue(3)
            p.fence()

        with nc.Block() as block:
            p.emit(block, sems)

    return nc


_NC_CACHE = [None]


def kernel(L0_attn_x, L0_attn_Wq, L0_attn_Wk, L0_attn_Wv, L0_attn_Wo,
           L0_ffn_weight_up, L0_ffn_weight_down, L0_ffn_norm_weight,
           L0_ffn_gamma):
    x = np.ascontiguousarray(np.asarray(L0_attn_x, dtype=np.float32))
    wq = np.ascontiguousarray(np.asarray(L0_attn_Wq, dtype=np.float32))
    wk = np.ascontiguousarray(np.asarray(L0_attn_Wk, dtype=np.float32))
    wv = np.ascontiguousarray(np.asarray(L0_attn_Wv, dtype=np.float32))
    wo = np.ascontiguousarray(np.asarray(L0_attn_Wo, dtype=np.float32))
    wup = np.ascontiguousarray(np.asarray(L0_ffn_weight_up, dtype=np.float32))
    wdn = np.ascontiguousarray(np.asarray(L0_ffn_weight_down, dtype=np.float32))
    nw = np.ascontiguousarray(
        np.asarray(L0_ffn_norm_weight, dtype=np.float32).reshape(1, D))
    gam = np.ascontiguousarray(
        np.asarray(L0_ffn_gamma, dtype=np.float32).reshape(1, D))

    if _NC_CACHE[0] is None:
        _NC_CACHE[0] = build()
    nc = _NC_CACHE[0]

    in_maps = []
    for c in range(8):
        in_maps.append({
            "x": np.ascontiguousarray(x[c]),
            "wq": wq, "wk": wk, "wv": wv, "wo": wo,
            "wup": wup, "wdn": wdn, "nw": nw, "gam": gam,
        })
    res = run_bass_kernel_spmd(nc, in_maps, core_ids=list(range(8)))
    return np.stack([r["out"] for r in res.results], axis=0)


if __name__ == "__main__":
    nc = build()
    print("build ok")


# version 17 (final)
# speedup vs baseline: 1.4658x; 1.4658x over previous
import math
import numpy as np
import concourse.bass as bass
import concourse.mybir as mybir
from concourse.bass_utils import run_bass_kernel_spmd
from contextlib import ExitStack

F32 = mybir.dt.float32
BF16 = mybir.dt.bfloat16
I32 = mybir.dt.int32
AF = mybir.ActivationFunctionType
ALU = mybir.AluOpType
AX = mybir.AxisListType

T = 2048
D = 2048
FF = 8192
NT = 16
SQ = math.sqrt(128.0)
ENGS = ("tensor", "vector", "scalar", "gpsimd", "sync")


def slope(h):
    return 2.0 ** (-(h // 4 + 1) / 2.0)


class Prog:
    def __init__(self):
        self.items = {e: [] for e in ENGS}
        self.cnt = {}
        self.res = {}

    def now(self, sem):
        return self.cnt.get(sem, 0)

    def w(self, sem):
        return (sem, self.now(sem))

    def use(self, name):
        return self.res.get(name, ("gdn", 0))

    def done(self, name, sem):
        self.res[name] = (sem, self.now(sem))

    def op(self, eng, fn, waits=(), inc=None, amt=1):
        self.items[eng].append((list(waits), fn, inc, amt))
        if inc is not None:
            self.cnt[inc] = self.cnt.get(inc, 0) + amt

    def vbub(self, waits=(), inc=None):
        self.op("vector",
                lambda e: e.tensor_copy(self.bubA[:], self.bubB[:]),
                waits, inc=inc)

    def dma(self, qeng, sem, out, in_, waits=()):
        self.op(qeng, lambda e, o=out, i=in_: e.dma_start(out=o, in_=i),
                waits, inc=sem, amt=16)

    def fence(self):
        snap = [(s, v) for s, v in self.cnt.items()]
        for e in ENGS:
            self.items[e].append((snap, None, None, 0))

    def emit(self, block, sems):
        def run(eng_obj, name):
            emitted = {}
            for waits, fn, inc, amt in self.items[name]:
                for s, v in waits:
                    if v > 0 and emitted.get(s, -1) < v:
                        eng_obj.wait_ge(sems[s], v)
                        emitted[s] = v
                if fn is not None:
                    instr = fn(eng_obj)
                    if inc is not None:
                        instr.then_inc(sems[inc], amt)

        @block.tensor
        def _(e):
            run(e, "tensor")

        @block.vector
        def _(e):
            run(e, "vector")

        @block.scalar
        def _(e):
            run(e, "scalar")

        @block.gpsimd
        def _(e):
            run(e, "gpsimd")

        @block.sync
        def _(e):
            run(e, "sync")


def build():
    nc = bass.Bass("TRN2", target_bir_lowering=False, debug=False)

    x_h = nc.dram_tensor("x", [T, D], F32, kind="ExternalInput").ap()
    wq_h = nc.dram_tensor("wq", [D, D], F32, kind="ExternalInput").ap()
    wk_h = nc.dram_tensor("wk", [D, 512], F32, kind="ExternalInput").ap()
    wv_h = nc.dram_tensor("wv", [D, 512], F32, kind="ExternalInput").ap()
    wo_h = nc.dram_tensor("wo", [D, D], F32, kind="ExternalInput").ap()
    wup_h = nc.dram_tensor("wup", [FF, D], F32, kind="ExternalInput").ap()
    wdn_h = nc.dram_tensor("wdn", [D, FF], F32, kind="ExternalInput").ap()
    nw_h = nc.dram_tensor("nw", [1, D], F32, kind="ExternalInput").ap()
    gam_h = nc.dram_tensor("gam", [1, D], F32, kind="ExternalInput").ap()
    out_h = nc.dram_tensor("out", [T, D], F32, kind="ExternalOutput").ap()

    xT_h = nc.dram_tensor("xTs", [D, T], F32, kind="Internal").ap()
    yT_h = nc.dram_tensor("yTs", [D, T], F32, kind="Internal").ap()
    x1_h = nc.dram_tensor("x1s", [T, D], F32, kind="Internal").ap()
    wupT_h = nc.dram_tensor("wupT", [D, FF], BF16, kind="Internal").ap()
    wdnT_h = nc.dram_tensor("wdnT", [FF, D], BF16, kind="Internal").ap()
    vecs_h = nc.dram_tensor("vecs", [1, T], F32, kind="Internal").ap()
    scal_h = nc.dram_tensor("scal", [1, 2], F32, kind="Internal").ap()

    xT3 = xT_h.rearrange("(c p) t -> p c t", p=128)
    wupT3 = wupT_h.rearrange("(c p) h -> p c h", p=128)
    wdnT3 = wdnT_h.rearrange("(b p) c -> p b c", p=128)
    vecs2 = vecs_h.rearrange("a (b c) -> (a b) c", b=16)

    p = Prog()
    SEMS = ("qld", "gld", "tdn", "vdn", "adn", "gdn", "sld")

    ctx = ExitStack()
    with ctx:
        sb = lambda name, shape, dt: ctx.enter_context(nc.sbuf_tensor(name, shape, dt))
        sems = {s: ctx.enter_context(nc.semaphore(s)) for s in SEMS}

        ident = sb("ident", [128, 128], F32)
        identb = sb("identb", [128, 128], BF16)
        zft = sb("zft", [128, 128], F32)
        zbt = sb("zbt", [128, 128], BF16)
        p.bubA = sb("bubA", [128, 1], F32)
        p.bubB = sb("bubB", [128, 1], F32)
        ones = sb("ones", [128, 1], F32)
        epsln = sb("epsln", [128, 1], F32)
        epsrms = sb("epsrms", [128, 1], F32)
        STNAMES = ["m0", "m1", "m2", "m3", "bias0", "bias1", "bias2",
                   "bias3", "sig0", "sig1", "sig2", "sig3",
                   "r0", "r1", "r2", "r3",
                   "sp0", "sp1", "sp2", "sp3", "sums", "nm", "ssq", "t1",
                   "m2t", "var", "sd", "rstd", "gmax", "gclp", "rg", "c",
                   "sx", "tstat", "waccu", "waccd",
                   "sq0", "sq1", "sq2", "sq3"]
        stt = {n: sb("st_" + n, [128, 1], F32) for n in STNAMES}
        mean_bc = [sb(f"mean_bc{i}", [128, 1], F32) for i in range(2)]
        sval_bc = [sb(f"sval_bc{i}", [128, 1], F32) for i in range(2)]
        thrp_bc = [sb(f"thrp_bc{i}", [128, 1], F32) for i in range(2)]
        thrn_bc = [sb(f"thrn_bc{i}", [128, 1], F32) for i in range(2)]
        mean_sb = sb("mean_sb", [1, 2], F32)
        sxs_stash = sb("sxs_stash", [128, 16], F32)
        banks = [ctx.enter_context(nc.psum_tensor(f"pb{i}", [128, 512], F32))
                 for i in range(6)]

        PARN = ("m", "bias", "sig", "r")

        def sc(name, par=0):
            key = f"{name}{par}" if name in PARN else name
            return stt[key][:]

        def gop(fn, waits=()):
            p.op("gpsimd", fn, waits, inc="gdn")

        gop(lambda e: e.memset(ones[:], 1.0))
        gop(lambda e: e.memset(p.bubB[:], 0.0))
        gop(lambda e: e.memset(epsln[:], 1e-5))
        gop(lambda e: e.memset(epsrms[:], 1e-6))
        gop(lambda e: e.memset(zft[:], 0.0))
        gop(lambda e: e.memset(zbt[:], 0.0))
        for t_ in (ident, identb):
            gop(lambda e, t2=t_: e.memset(t2[:], 1.0))
            gop(lambda e, t2=t_: e.affine_select(
                t2[:], t2[:], [[-1, 128]], ALU.is_ge, 0.0,
                base=0, channel_multiplier=1))
            gop(lambda e, t2=t_: e.affine_select(
                t2[:], t2[:], [[1, 128]], ALU.is_ge, 0.0,
                base=0, channel_multiplier=-1))

        bf_ctx = ExitStack()
        with bf_ctx:
            wstage = [bf_ctx.enter_context(
                nc.sbuf_tensor(f"wst{i}", [128, 2048], F32)) for i in range(4)]
            sxrow = bf_ctx.enter_context(
                nc.sbuf_tensor("sxrow", [16, 128], F32))

            actx = ExitStack()
            with actx:
                asb = lambda name, shape, dt: actx.enter_context(
                    nc.sbuf_tensor(name, shape, dt))
                Ccol = asb("Ccol", [128, T], F32)
                maskb = asb("maskb", [128, 896], F32)
                K_all = asb("K_all", [128, 4 * T], F32)
                V_all = asb("V_all", [128, 16 * 512], F32)
                Qpair = asb("Qpair", [128, 2 * T], F32)
                Ssb = [asb(f"Ssb{i}", [128, T], F32) for i in range(3)]
                Ssb.append(wstage[3])
                Psb = asb("Psb", [128, T], F32)
                dtmp = asb("dtmp", [128, 512], F32)
                PTb2 = [asb(f"PTb{i}", [128, 16 * 128], F32)
                        for i in range(2)]
                yh = [asb(f"yh{i}", [128, T], F32) for i in range(2)]
                wqst = [asb(f"wqst{i}", [128, 256], F32) for i in range(2)]
                estage = [asb(f"est{i}", [128, 1024], F32) for i in range(2)]
                wkvst = [estage[i][:, 0:512] for i in range(2)]
                ptp = actx.enter_context(
                    nc.psum_tensor("ptp", [128, 1024], F32))
                ptp3 = ptp[:].rearrange("p (a b) -> p a b", a=2)
                PTb3 = [t[:].rearrange("p (k ab) -> p k ab", ab=128)
                        for t in PTb2]

                gop(lambda e: e.memset(K_all[:], 0.0))
                gop(lambda e: e.memset(V_all[:], 0.0))
                gop(lambda e: e.iota(Ccol[:], [[1, T]], base=0,
                                     channel_multiplier=0,
                                     allow_small_or_imprecise_dtypes=True))
                gop(lambda e: e.memset(maskb[:], 0.0))
                gop(lambda e: e.affine_select(
                    maskb[:], maskb[:], [[-1, 896]], ALU.is_ge, -1e33,
                    base=384, channel_multiplier=1))
                A_DONE = p.w("gdn")

                for ti in range(NT):
                    slot = wstage[ti % 2]
                    p.dma("gpsimd", "gld", slot[:],
                          x_h[ti * 128:(ti + 1) * 128, :],
                          waits=[p.use(f"wst{ti % 2}"), A_DONE])
                    ld = p.w("gld")
                    half = PTb2[ti % 2]
                    for dc in range(16):
                        bk = banks[dc % 2]
                        p.op("tensor",
                             lambda e, s=slot, d=dc, b=bk: e.transpose(
                                 b[:, 0:128], s[:, d * 128:(d + 1) * 128],
                                 ident[:]),
                             waits=[ld, p.use(f"bank{dc % 2}"), A_DONE],
                             inc="tdn")
                        p.op("vector",
                             lambda e, b=bk, hf=half, d=dc: e.tensor_copy(
                                 hf[:][:, d * 128:(d + 1) * 128], b[:, 0:128]),
                             waits=[p.w("tdn"), p.use(f"ptbh{ti % 2}")],
                             inc="vdn")
                        p.done(f"bank{dc % 2}", "vdn")
                    p.done(f"wst{ti % 2}", "tdn")
                    p.dma("gpsimd", "gld", xT3[:, :, ti * 128:(ti + 1) * 128],
                          half[:].rearrange("p (c t) -> p c t", c=16),
                          waits=[p.w("vdn")])
                    p.done(f"ptbh{ti % 2}", "gld")
                p.fence()

                for pi in range(4):
                    kv = pi
                    for dc in range(16):
                        si = dc % 3
                        xslot = wstage[si]
                        p.dma("sync", "qld", xslot[:],
                              xT_h[dc * 128:(dc + 1) * 128, :],
                              waits=[p.use(f"wst{si}")])
                        wslot = wkvst[dc % 2]
                        p.dma("scalar", "sld", wslot,
                              wv_h[dc * 128:(dc + 1) * 128, :],
                              waits=[p.use(f"est{dc % 2}")])
                        qslot = wqst[dc % 2]
                        p.dma("scalar", "sld", qslot[:, 0:128],
                              wk_h[dc * 128:(dc + 1) * 128,
                                   kv * 128:(kv + 1) * 128],
                              waits=[p.use(f"wqst{dc % 2}")])
                        ld = p.w("qld")
                        lds = p.w("sld")
                        first, last = dc == 0, dc == 15
                        for kt in range(4):
                            bw = [p.use(f"bank{kt}")] if first else []
                            p.op("tensor",
                                 lambda e, q=qslot, xs=xslot, k=kt, f=first,
                                 l=last: e.matmul(
                                     banks[k][:], q[:, 0:128],
                                     xs[:, k * 512:(k + 1) * 512],
                                     start=f, stop=l),
                                 waits=[ld, lds] + bw)
                        for vt in range(4):
                            ti = pi * 4 + vt
                            bname = (f"bank{4 + vt}" if vt < 2
                                     else f"ptph{vt - 2}")
                            dst = (banks[4][:] if vt == 0 else banks[5][:]
                                   if vt == 1 else
                                   ptp[:, (vt - 2) * 512:(vt - 1) * 512])
                            bw = [p.use(bname)] if first else []
                            p.op("tensor",
                                 lambda e, xs=xslot, w2=wslot, t=ti, d=dst,
                                 f=first, l=last: e.matmul(
                                     d, xs[:, t * 128:(t + 1) * 128],
                                     w2, start=f, stop=l),
                                 waits=[ld, lds] + bw,
                                 inc="tdn" if vt == 3 else None)
                        p.done(f"wst{si}", "tdn")
                        p.done(f"est{dc % 2}", "tdn")
                        p.done(f"wqst{dc % 2}", "tdn")
                    mmd = p.w("tdn")
                    for kt in range(4):
                        p.op("vector",
                             lambda e, k=kt, kvv=kv: e.tensor_copy(
                                 K_all[:, kvv * T + k * 512:
                                       kvv * T + (k + 1) * 512],
                                 banks[k][:]),
                             waits=[mmd], inc="vdn")
                        p.done(f"bank{kt}", "vdn")
                    for vt in range(4):
                        ti = pi * 4 + vt
                        bname = f"bank{4 + vt}" if vt < 2 else f"ptph{vt - 2}"
                        src = (banks[4][:] if vt == 0 else banks[5][:]
                               if vt == 1 else
                               ptp[:, (vt - 2) * 512:(vt - 1) * 512])
                        p.op("vector",
                             lambda e, s=src, t=ti: e.tensor_copy(
                                 V_all[:, t * 512:(t + 1) * 512], s),
                             waits=[mmd], inc="vdn")
                        p.done(bname, "vdn")
                p.fence()

                def emit_e1_chunk(ch):
                    si = ch % 2
                    slot = estage[si]
                    if ch < 128:
                        src = wup_h[(ch // 2) * 128:(ch // 2 + 1) * 128,
                                    (ch % 2) * 1024:(ch % 2 + 1) * 1024]
                        acc = sc("waccu")
                    else:
                        c2 = ch - 128
                        cc, qq = c2 // 8, c2 % 8
                        src = wdn_h[cc * 128:(cc + 1) * 128,
                                    qq * 1024:(qq + 1) * 1024]
                        acc = sc("waccd")
                    p.dma("gpsimd", "gld", slot[:], src,
                          waits=[p.use(f"est{si}")])
                    ld = p.w("gld")
                    p.op("vector",
                         lambda e, s=slot: e.reduce_sum(
                             sc("tstat"), s[:], axis=AX.X,
                             apply_absolute_value=True),
                         waits=[ld], inc="vdn")
                    p.done(f"est{si}", "vdn")
                    p.vbub()
                    if ch % 128 == 0:
                        p.op("vector",
                             lambda e, a=acc: e.tensor_copy(a, sc("tstat")),
                             waits=[], inc="vdn")
                    else:
                        p.op("vector",
                             lambda e, a=acc: e.scalar_tensor_tensor(
                                 a, sc("tstat"), 1.0, a, ALU.mult, ALU.add),
                             waits=[], inc="vdn")

                for pr in range(8):
                    for dc in range(16):
                        si = dc % 3
                        xslot = wstage[si]
                        p.dma("sync", "qld", xslot[:],
                              xT_h[dc * 128:(dc + 1) * 128, :],
                              waits=[p.use(f"wst{si}")])
                        qslot = wqst[dc % 2]
                        p.dma("scalar", "sld", qslot[:],
                              wq_h[dc * 128:(dc + 1) * 128,
                                   pr * 256:(pr + 1) * 256],
                              waits=[p.use(f"wqst{dc % 2}")])
                        ld = p.w("qld")
                        lds = p.w("sld")
                        first, last = dc == 0, dc == 15
                        for gi in range(8):
                            hh, tg = gi // 4, gi % 4
                            bname = f"bank{gi}" if gi < 6 else f"ptph{gi - 6}"
                            dst = (banks[gi][:] if gi < 6
                                   else ptp[:, (gi - 6) * 512:(gi - 5) * 512])
                            bw = [p.use(bname)] if first else []
                            p.op("tensor",
                                 lambda e, q=qslot, xs=xslot, h2=hh, t=tg,
                                 d=dst, f=first, l=last: e.matmul(
                                     d, q[:, h2 * 128:(h2 + 1) * 128],
                                     xs[:, t * 512:(t + 1) * 512],
                                     start=f, stop=l),
                                 waits=[ld, lds] + bw,
                                 inc="tdn" if gi == 7 else None)
                        p.done(f"wst{si}", "tdn")
                        p.done(f"wqst{dc % 2}", "tdn")
                    qmm = p.w("tdn")
                    for gi in range(8):
                        hh, tg = gi // 4, gi % 4
                        scl = 1.0 / (SQ * slope(2 * pr + hh))
                        bname = f"bank{gi}" if gi < 6 else f"ptph{gi - 6}"
                        src = (banks[gi][:] if gi < 6
                               else ptp[:, (gi - 6) * 512:(gi - 5) * 512])
                        p.op("vector",
                             lambda e, s=src, h2=hh, t=tg, sc2=scl:
                             e.tensor_scalar_mul(
                                 Qpair[:, h2 * T + t * 512:
                                       h2 * T + (t + 1) * 512], s, sc2),
                             waits=[qmm, p.use("Qpair")], inc="vdn")
                        p.done(bname, "vdn")
                    QEV = p.w("vdn")

                    kv = (2 * pr) // 4
                    s_h = slope(2 * pr)
                    chain_done = {}

                    def emit_scores(hh, qi, par):
                        nkt = qi // 4 + 1
                        kpad = 512 * nkt
                        dkt = qi // 4
                        mvar = qi % 4
                        smm = []
                        for kt in range(nkt):
                            p.op("tensor",
                                 lambda e, k=kt, q=qi, h2=hh, kvv=kv:
                                 e.matmul(
                                     banks[k][:],
                                     Qpair[:, h2 * T + q * 128:
                                           h2 * T + (q + 1) * 128],
                                     K_all[:, kvv * T + k * 512:
                                           kvv * T + (k + 1) * 512],
                                     start=True, stop=True),
                                 waits=[QEV, p.use(f"bank{kt}")],
                                 inc="tdn")
                            smm.append(p.w("tdn"))
                        for kt in range(nkt):
                            if kt == dkt:
                                p.op("vector",
                                     lambda e, k=kt: e.tensor_add(
                                         dtmp[:], banks[k][:],
                                         Ccol[:, k * 512:(k + 1) * 512]),
                                     waits=[smm[kt], p.use(f"Ssb{par}")],
                                     inc="vdn")
                                p.op("vector",
                                     lambda e, k=kt, pa=par, v=mvar:
                                     e.tensor_add(
                                         Ssb[pa][:, k * 512:(k + 1) * 512],
                                         dtmp[:],
                                         maskb[:, 384 - 128 * v:
                                               896 - 128 * v]),
                                     waits=[], inc="vdn")
                            else:
                                p.op("vector",
                                     lambda e, k=kt, pa=par: e.tensor_add(
                                         Ssb[pa][:, k * 512:(k + 1) * 512],
                                         banks[k][:],
                                         Ccol[:, k * 512:(k + 1) * 512]),
                                     waits=[smm[kt], p.use(f"Ssb{par}")],
                                     inc="vdn")
                            p.done(f"bank{kt}", "vdn")
                        p.op("vector",
                             lambda e, pa=par, kp=kpad: e.reduce_max(
                                 sc("m", pa), Ssb[pa][:, 0:kp], axis=AX.X),
                             waits=[], inc="vdn")
                        p.vbub()
                        p.op("vector",
                             lambda e, pa=par, s2=s_h: e.tensor_scalar_mul(
                                 sc("bias", pa), sc("m", pa), -s2),
                             waits=[], inc="vdn")
                        p.vbub()
                        p.vbub(inc="vdn")
                        vd = p.w("vdn")
                        p.op("scalar",
                             lambda e, pa=par, kp=kpad, s2=s_h: e.activation(
                                 Psb[:, 0:kp], Ssb[pa][:, 0:kp], AF.Exp,
                                 bias=sc("bias", pa), scale=s2,
                                 accum_out=sc("sig", pa)),
                             waits=[vd], inc="adn")
                        ad = p.w("adn")
                        p.vbub(waits=[ad])
                        p.vbub()
                        p.op("vector",
                             lambda e, pa=par: e.reciprocal(
                                 sc("r", pa), sc("sig", pa)),
                             waits=[], inc="vdn")
                        p.vbub()
                        p.op("vector",
                             lambda e, pa=par, kp=kpad: e.tensor_scalar(
                                 Ssb[pa][:, 0:kp], Psb[:, 0:kp],
                                 sc("r", pa), None, ALU.mult),
                             waits=[], inc="vdn")
                        chain_done[(hh, qi)] = p.w("vdn")

                    def emit_pt_pv(hh, qi, par):
                        nkt = qi // 4 + 1
                        kpad = 512 * nkt
                        nkb = kpad // 128
                        pnd = chain_done[(hh, qi)]
                        REGN = ("ptph0", "ptph1", "bank5")
                        for kb in range(nkb):
                            reg = REGN[kb % 3]
                            dst = (ptp[:, 0:128] if kb % 3 == 0 else
                                   ptp[:, 512:640] if kb % 3 == 1 else
                                   banks[5][:, 0:128])
                            p.op("tensor",
                                 lambda e, pa=par, k=kb, d2=dst:
                                 e.transpose(
                                     d2, Ssb[pa][:, k * 128:(k + 1) * 128],
                                     ident[:]),
                                 waits=[pnd, p.use(reg),
                                        p.use(f"PTb{hh}")],
                                 inc="tdn")
                            td = p.w("tdn")
                            p.op("vector",
                                 lambda e, k=kb, h2=hh, d2=dst:
                                 e.tensor_copy(
                                     PTb2[h2][:, k * 128:(k + 1) * 128], d2),
                                 waits=[td], inc="vdn")
                            p.done(reg, "vdn")
                        p.done(f"Ssb{par}", "tdn")
                        ptev = p.w("vdn")
                        for kc in range(nkb):
                            p.op("tensor",
                                 lambda e, k=kc, h2=hh, kvv=kv,
                                 f=(kc == 0), l=(kc == nkb - 1):
                                 e.matmul(
                                     banks[4][:, 0:128],
                                     V_all[:, k * 512 + kvv * 128:
                                           k * 512 + (kvv + 1) * 128],
                                     PTb2[h2][:, k * 128:(k + 1) * 128],
                                     start=f, stop=l),
                                 waits=[ptev, p.use("bank4")],
                                 inc="tdn" if kc == nkb - 1 else None)
                        p.done(f"PTb{hh}", "tdn")
                        pvd = p.w("tdn")
                        p.op("vector",
                             lambda e, q=qi, h2=hh: e.tensor_copy(
                                 yh[h2][:, q * 128:(q + 1) * 128],
                                 banks[4][:, 0:128]),
                             waits=[pvd, p.use(f"yh{hh}")],
                             inc="vdn")
                        p.done("bank4", "vdn")

                    for idx in range(32):
                        hh, qi = idx % 2, idx // 2
                        emit_scores(hh, qi, idx % 4)
                        if idx >= 3:
                            j = idx - 3
                            emit_pt_pv(j % 2, j // 2, j % 4)
                        emit_e1_chunk(pr * 32 + idx)
                    for j in range(29, 32):
                        emit_pt_pv(j % 2, j // 2, j % 4)
                    for hh in range(2):
                        h = 2 * pr + hh
                        p.dma("gpsimd", "gld", yT_h[h * 128:(h + 1) * 128, :],
                              yh[hh][:], waits=[p.w("vdn")])
                        p.done(f"yh{hh}", "gld")
                    p.done("Qpair", "tdn")
                p.fence()

                for wi in range(2):
                    acc = sc("waccu") if wi == 0 else sc("waccd")
                    p.vbub()
                    p.vbub(inc="vdn")
                    vd = p.w("vdn")
                    p.op("tensor",
                         lambda e, a=acc: e.matmul(banks[0][0:1, 0:1], a,
                                                   ones[:], start=True,
                                                   stop=True),
                         waits=[vd], inc="tdn")
                    td = p.w("tdn")
                    p.op("vector",
                         lambda e, w2=wi: e.tensor_scalar_mul(
                             mean_sb[0:1, w2:w2 + 1], banks[0][0:1, 0:1],
                             1.0 / (FF * D)),
                         waits=[td], inc="vdn")
                p.dma("gpsimd", "gld", scal_h[:], mean_sb[:],
                      waits=[p.w("vdn")])
                for w2 in range(2):
                    p.dma("sync", "qld", mean_bc[w2][:],
                          scal_h[:, w2:w2 + 1].partition_broadcast(128),
                          waits=[p.w("gld")])
                mbcl = p.w("qld")
                for w2 in range(2):
                    p.op("vector",
                         lambda e, i=w2: e.tensor_scalar_max(
                             sval_bc[i][:], mean_bc[i][:], 1e-8),
                         waits=[mbcl], inc="vdn")
                    p.vbub()
                    p.op("vector",
                         lambda e, i=w2: e.tensor_scalar_mul(
                             thrp_bc[i][:], sval_bc[i][:], 0.5),
                         waits=[], inc="vdn")
                    p.vbub()
                    p.op("vector",
                         lambda e, i=w2: e.tensor_scalar_mul(
                             thrn_bc[i][:], thrp_bc[i][:], -1.0),
                         waits=[], inc="vdn")
                p.vbub()
                p.vbub(inc="vdn")
                THRLD = p.w("vdn")
                p.fence()

            xqT = ctx.enter_context(
                nc.sbuf_tensor("xqT", [128, 16 * 2048], BF16, side="right"))

            e2f_ctx = ExitStack()
            with e2f_ctx:
                esb = lambda name, shape, dt: e2f_ctx.enter_context(
                    nc.sbuf_tensor(name, shape, dt))
                tern1 = esb("tern1", [128, 1024], F32)
                tern2 = esb("tern2", [128, 1024], F32)
                ternf = esb("ternf", [128, 1024], F32)
                tstage = [esb(f"tstage{i}", [128, 1024], BF16) for i in range(2)]
                estg = [esb(f"e2st{i}", [128, 1024], F32) for i in range(2)]
                xres = [esb(f"xres{i}", [128, 2048], F32) for i in range(2)]
                f1 = esb("f1", [128, 2048], F32)
                f2 = esb("f2", [128, 2048], F32)
                f3 = esb("f3", [128, 2048], F32)
                f4 = esb("f4", [128, 2048], F32)
                f5 = [esb(f"f5_{i}", [128, 2048], F32) for i in range(2)]
                i32t = esb("i32t", [128, 2048], I32)
                nw_rep = esb("nw_rep", [128, 2048], F32)
                fps = e2f_ctx.enter_context(
                    nc.psum_tensor("fps", [128, 1024], F32))

                def fbank(g):
                    return (banks[g][:] if g < 6
                            else fps[:, (g - 6) * 512:(g - 5) * 512])

                p.dma("sync", "qld", nw_rep[:],
                      nw_h.partition_broadcast(128), waits=[])
                NWLD = p.w("qld")

                def emit_e2_unit(u):
                    si = u % 2
                    slot = estg[si]
                    ts = tstage[si]
                    if u < 128:
                        rb, hh2 = u // 2, u % 2
                        src2 = wup_h[rb * 128:(rb + 1) * 128,
                                     hh2 * 1024:(hh2 + 1) * 1024]
                        wi2 = 0
                        dst3 = wupT3[:, hh2 * 8:(hh2 + 1) * 8,
                                     rb * 128:(rb + 1) * 128]
                    else:
                        c2 = u - 128
                        cc, qq = c2 // 8, c2 % 8
                        src2 = wdn_h[cc * 128:(cc + 1) * 128,
                                     qq * 1024:(qq + 1) * 1024]
                        wi2 = 1
                        dst3 = wdnT3[:, qq * 8:(qq + 1) * 8,
                                     cc * 128:(cc + 1) * 128]
                    p.dma("gpsimd", "gld", slot[:], src2,
                          waits=[p.use(f"e2st{si}")])
                    ldq = p.w("gld")
                    p.op("vector",
                         lambda e, s=slot, i=wi2: e.tensor_scalar(
                             tern1[:], s[:], thrp_bc[i][:], None, ALU.is_ge),
                         waits=[ldq, THRLD], inc="vdn")
                    p.op("vector",
                         lambda e, s=slot, i=wi2: e.tensor_scalar(
                             tern2[:], s[:], thrn_bc[i][:], None, ALU.is_le),
                         waits=[], inc="vdn")
                    p.done(f"e2st{si}", "vdn")
                    p.op("vector",
                         lambda e: e.tensor_sub(ternf[:], tern1[:], tern2[:]),
                         waits=[p.use("ternf")], inc="vdn")
                    t0 = p.w("vdn")
                    for b in range(8):
                        bi = b % 2
                        p.op("tensor",
                             lambda e, b2=b, i=bi: e.transpose(
                                 banks[i][:, 0:128],
                                 ternf[:, b2 * 128:(b2 + 1) * 128],
                                 ident[:]),
                             waits=[t0, p.use(f"bank{bi}")], inc="tdn")
                        p.op("vector",
                             lambda e, b2=b, i=bi, t=ts: e.tensor_copy(
                                 t[:, b2 * 128:(b2 + 1) * 128],
                                 banks[i][:, 0:128]),
                             waits=[p.w("tdn")] +
                             ([p.use(f"tstage{si}")] if b == 0 else []),
                             inc="vdn")
                        p.done(f"bank{bi}", "vdn")
                    p.done("ternf", "tdn")
                    p.dma("gpsimd", "gld", dst3,
                          ts[:].rearrange("p (b c) -> p b c", b=8),
                          waits=[p.w("vdn")])
                    p.done(f"tstage{si}", "gld")

                xq_pend = []
                e2_next = [0]

                def emit_xq_transpose(ti, QED):
                    for dc in range(16):
                        bk = banks[dc % 2]
                        p.op("tensor",
                             lambda e, d=dc, b=bk, t=ti: e.transpose(
                                 b[:, 0:128],
                                 f5[t % 2][:, d * 128:(d + 1) * 128],
                                 ident[:]),
                             waits=[QED, p.use(f"bank{dc % 2}")],
                             inc="tdn")
                        p.op("vector",
                             lambda e, d=dc, b=bk, t=ti: e.tensor_scalar(
                                 xqT[:, d * 2048 + t * 128:
                                     d * 2048 + (t + 1) * 128],
                                 b[:, 0:128], 127.0, -127.0,
                                 ALU.min, ALU.max),
                             waits=[p.w("tdn")], inc="vdn")
                        p.done(f"bank{dc % 2}", "vdn")
                    p.done(f"f5_{ti % 2}", "tdn")

                for tp in range(8):
                    for t2 in range(2):
                        ti = tp * 2 + t2
                        p.dma("gpsimd", "gld", xres[t2][:],
                              x_h[ti * 128:(ti + 1) * 128, :],
                              waits=[p.use(f"xres{t2}")])
                    XR = p.w("gld")
                    for fc in range(16):
                        yi = 2 * (fc % 2)
                        yst = wstage[yi]
                        wst = wstage[yi + 1]
                        p.dma("sync", "qld", yst[:],
                              yT_h[fc * 128:(fc + 1) * 128, :],
                              waits=[p.use(f"wst{yi}")])
                        p.dma("scalar", "sld", wst[:],
                              wo_h[fc * 128:(fc + 1) * 128, :],
                              waits=[p.use(f"wst{yi + 1}")])
                        ld = p.w("qld")
                        lds = p.w("sld")
                        first, last = fc == 0, fc == 15
                        for gi in range(8):
                            t2, ct = gi // 4, gi % 4
                            bname = f"bank{gi}" if gi < 6 else f"fps{gi - 6}"
                            bw = [p.use(bname)] if first else []
                            p.op("tensor",
                                 lambda e, y=yst, w2=wst, tt=t2, c=ct,
                                 g=gi, f=first, l=last, t0=tp:
                                 e.matmul(
                                     fbank(g),
                                     y[:, (t0 * 2 + tt) * 128:
                                       (t0 * 2 + tt + 1) * 128],
                                     w2[:, c * 512:(c + 1) * 512],
                                     start=f, stop=l),
                                 waits=[ld, lds] + bw,
                                 inc="tdn" if gi == 7 else None)
                        p.done(f"wst{yi}", "tdn")
                        p.done(f"wst{yi + 1}", "tdn")
                    OMM = p.w("tdn")
                    for t2 in range(2):
                        for ct in range(4):
                            gi = t2 * 4 + ct
                            bname = f"bank{gi}" if gi < 6 else f"fps{gi - 6}"
                            p.op("vector",
                                 lambda e, g=gi, c=ct, xr=xres[t2], t4=t2:
                                 e.scalar_tensor_tensor(
                                     f1[:, c * 512:(c + 1) * 512]
                                     if t4 == 0 else
                                     f2[:, c * 512:(c + 1) * 512],
                                     fbank(g), 1.0,
                                     xr[:, c * 512:(c + 1) * 512],
                                     ALU.mult, ALU.add,
                                     accum_out=sc("sp%d" % c) if t4 == 0
                                     else sc("sq%d" % c)),
                                 waits=[OMM, XR], inc="vdn")
                            p.done(bname, "vdn")
                        p.done(f"xres{t2}", "vdn")
                    for ti_, QED_ in xq_pend:
                        emit_xq_transpose(ti_, QED_)
                    xq_pend.clear()
                    for _ in range(32):
                        if e2_next[0] < 256:
                            emit_e2_unit(e2_next[0])
                            e2_next[0] += 1
                    for t2 in range(2):
                        ti = tp * 2 + t2
                        fin = f1 if t2 == 0 else f2
                        ac = "sp" if t2 == 0 else "sq"
                        p.vbub()
                        p.op("vector", lambda e, a=ac: e.tensor_add(
                            sc("sums"), sc(a + "0"), sc(a + "1")),
                            waits=[], inc="vdn")
                        p.op("vector", lambda e, a=ac: e.tensor_add(
                            sc("t1"), sc(a + "2"), sc(a + "3")),
                            waits=[], inc="vdn")
                        p.vbub()
                        p.op("vector", lambda e: e.tensor_add(
                            sc("sums"), sc("sums"), sc("t1")),
                            waits=[], inc="vdn")
                        p.vbub()
                        p.op("vector", lambda e: e.tensor_scalar_mul(
                            sc("nm"), sc("sums"), -1.0 / D),
                            waits=[], inc="vdn")
                        p.op("vector", lambda e, fi=fin: e.scalar_tensor_tensor(
                            f3[:], fi[:], 1.0, fi[:], ALU.mult, ALU.mult,
                            accum_out=sc("ssq")),
                            waits=[], inc="vdn")
                        p.vbub()
                        p.op("vector", lambda e: e.tensor_scalar_mul(
                            sc("t1"), sc("ssq"), 1.0 / D),
                            waits=[], inc="vdn")
                        p.op("vector", lambda e: e.tensor_mul(
                            sc("m2t"), sc("nm"), sc("nm")),
                            waits=[], inc="vdn")
                        p.vbub()
                        p.op("vector", lambda e: e.tensor_sub(
                            sc("var"), sc("t1"), sc("m2t")),
                            waits=[], inc="vdn")
                        p.vbub()
                        p.vbub(inc="vdn")
                        vd = p.w("vdn")
                        p.op("scalar", lambda e: e.activation(
                            sc("sd"), sc("var"), AF.Sqrt, bias=epsln[:]),
                            waits=[vd], inc="adn")
                        ad = p.w("adn")
                        p.vbub(waits=[ad])
                        p.vbub()
                        p.op("vector", lambda e: e.reciprocal(
                            sc("rstd"), sc("sd")), waits=[], inc="vdn")
                        p.op("vector", lambda e, fi=fin: e.tensor_scalar(
                            f3[:], fi[:], sc("nm"), None, ALU.add),
                            waits=[], inc="vdn")
                        p.op("vector", lambda e: e.tensor_scalar(
                            f4[:], f3[:], sc("rstd"), None, ALU.mult),
                            waits=[p.use("x1sp")], inc="vdn")
                        p.dma("gpsimd", "gld",
                              x1_h[ti * 128:(ti + 1) * 128, :], f4[:],
                              waits=[p.w("vdn")])
                        p.done("x1sp", "gld")
                        p.op("vector", lambda e: e.scalar_tensor_tensor(
                            f3[:], f4[:], 1.0, f4[:], ALU.mult, ALU.mult,
                            accum_out=sc("ssq")),
                            waits=[], inc="vdn")
                        p.vbub()
                        p.op("vector", lambda e: e.tensor_scalar_mul(
                            sc("t1"), sc("ssq"), 1.0 / D),
                            waits=[], inc="vdn")
                        p.vbub()
                        p.vbub(inc="vdn")
                        vd = p.w("vdn")
                        p.op("scalar", lambda e: e.activation(
                            sc("sd"), sc("t1"), AF.Sqrt, bias=epsrms[:]),
                            waits=[vd], inc="adn")
                        ad = p.w("adn")
                        p.vbub(waits=[ad])
                        p.vbub()
                        p.op("vector", lambda e: e.reciprocal(
                            sc("rstd"), sc("sd")), waits=[], inc="vdn")
                        p.vbub()
                        p.op("vector", lambda e: e.tensor_scalar(
                            f3[:], f4[:], sc("rstd"), None, ALU.mult),
                            waits=[], inc="vdn")
                        p.op("vector", lambda e, fi=fin: e.tensor_mul(
                            fi[:], f3[:], nw_rep[:]),
                            waits=[NWLD], inc="vdn")
                        p.op("vector", lambda e, fi=fin: e.reduce_max(
                            sc("gmax"), fi[:], axis=AX.X,
                            apply_absolute_value=True),
                            waits=[], inc="vdn")
                        p.vbub()
                        p.op("vector", lambda e: e.tensor_scalar_max(
                            sc("gclp"), sc("gmax"), 1e-8),
                            waits=[], inc="vdn")
                        p.vbub()
                        p.op("vector", lambda e: e.reciprocal(
                            sc("rg"), sc("gclp")), waits=[], inc="vdn")
                        p.vbub()
                        p.op("vector", lambda e: e.tensor_scalar_mul(
                            sc("c"), sc("rg"), 127.0), waits=[], inc="vdn")
                        p.op("vector", lambda e: e.tensor_scalar_mul(
                            sc("sx"), sc("gclp"), 1.0 / 127.0),
                            waits=[], inc="vdn")
                        p.vbub()
                        p.op("vector", lambda e, t=ti: e.tensor_scalar(
                            sxs_stash[:, t:t + 1], sc("sx"),
                            sval_bc[0][:], None, ALU.mult),
                            waits=[THRLD], inc="vdn")
                        p.op("vector", lambda e, fi=fin: e.tensor_scalar(
                            f3[:], fi[:], sc("c"), None, ALU.mult),
                            waits=[], inc="vdn")
                        p.op("vector", lambda e: e.tensor_copy(
                            i32t[:], f3[:]), waits=[], inc="vdn")
                        fp = f5[ti % 2]
                        p.op("vector", lambda e, fp=fp: e.tensor_copy(
                            fp[:], i32t[:]), waits=[p.use(f"f5_{ti % 2}")],
                            inc="vdn")
                        p.op("vector", lambda e, fp=fp: e.tensor_sub(
                            f4[:], f3[:], fp[:]), waits=[p.use("x1sp")],
                            inc="vdn")
                        p.op("vector", lambda e: e.tensor_single_scalar(
                            f3[:], f4[:], 0.5, ALU.is_ge),
                            waits=[], inc="vdn")
                        p.op("vector", lambda e, fp=fp: e.tensor_add(
                            fp[:], fp[:], f3[:]), waits=[], inc="vdn")
                        p.op("vector", lambda e: e.tensor_single_scalar(
                            f3[:], f4[:], -0.5, ALU.is_lt),
                            waits=[], inc="vdn")
                        p.op("vector", lambda e, fp=fp: e.tensor_sub(
                            fp[:], fp[:], f3[:]), waits=[], inc="vdn")
                        xq_pend.append((ti, p.w("vdn")))
                for ti, QED in xq_pend:
                    emit_xq_transpose(ti, QED)
                for u in range(e2_next[0], 256):
                    emit_e2_unit(u)
                sxv = p.w("vdn")
                p.op("tensor", lambda e: e.transpose(
                    banks[0][0:16, 0:128], sxs_stash[:], ident[:]),
                    waits=[sxv, p.use("bank0")], inc="tdn")
                p.op("vector", lambda e: e.tensor_copy(
                    sxrow[:], banks[0][0:16, 0:128]),
                    waits=[p.w("tdn")], inc="vdn")
                p.dma("gpsimd", "gld", vecs2[:], sxrow[:],
                      waits=[p.w("vdn")])
                p.fence()

        gctx = ExitStack()
        with gctx:
            gsb = lambda name, shape, dt: gctx.enter_context(
                nc.sbuf_tensor(name, shape, dt))
            sxg = [gsb(f"sxg{i}", [128, 512], F32) for i in range(2)]
            gam_rep = gsb("gam_rep", [128, 2048], F32)
            hfhi = gsb("hfhi", [128, 8 * 512], BF16)
            hflo = gsb("hflo", [128, 8 * 512], BF16)
            oacc2 = [gsb(f"oacc{i}", [128, 4 * 2048], F32) for i in range(2)]
            wub = [gsb(f"wub{i}", [128, 8 * 128], BF16) for i in range(2)]
            wdnb = [gsb(f"wdnb{i}", [128, 2048], BF16) for i in range(2)]
            g1 = gsb("g1", [128, 2048], F32)
            g2 = gsb("g2", [128, 2048], F32)
            sfu = gsb("sfu", [128, 512], F32)
            hbuf = gsb("hbuf", [128, 512], F32)
            x1r = [gsb(f"x1r{i}", [128, 2048], F32) for i in range(2)]
            gps = gctx.enter_context(nc.psum_tensor("gps", [128, 1024], F32))

            def gbank(g):
                return (banks[g][:] if g < 6
                        else gps[:, (g - 6) * 512:(g - 5) * 512])

            p.dma("sync", "qld", gam_rep[:],
                  gam_h.partition_broadcast(128), waits=[])
            GAMLD = p.w("qld")

            def emit_epilogue(tg2):
                oc = oacc2[tg2 % 2]
                for tt in range(4):
                    ti = tg2 * 4 + tt
                    p.dma("gpsimd", "gld", x1r[tt % 2][:],
                          x1_h[ti * 128:(ti + 1) * 128, :],
                          waits=[p.use(f"x1r{tt % 2}")])
                    X1L = p.w("gld")
                    p.op("vector",
                         lambda e, t3=tt, o=oc: e.tensor_mul(
                             g1[:], o[:, t3 * 2048:(t3 + 1) * 2048],
                             gam_rep[:]),
                         waits=[GAMLD, p.use("g1")], inc="vdn")
                    p.op("vector", lambda e: e.tensor_scalar(
                        g2[:], g1[:], sval_bc[1][:], None, ALU.mult),
                        waits=[], inc="vdn")
                    p.op("vector",
                         lambda e, t3=tt: e.scalar_tensor_tensor(
                             g1[:], g2[:], 1.0, x1r[t3 % 2][:],
                             ALU.mult, ALU.add, accum_out=sc("sums")),
                         waits=[X1L], inc="vdn")
                    p.done(f"x1r{tt % 2}", "vdn")
                    p.op("vector", lambda e: e.scalar_tensor_tensor(
                        g2[:], g1[:], 1.0, g1[:], ALU.mult, ALU.mult,
                        accum_out=sc("ssq")),
                        waits=[], inc="vdn")
                    p.vbub()
                    p.op("vector", lambda e: e.tensor_scalar_mul(
                        sc("nm"), sc("sums"), -1.0 / D),
                        waits=[], inc="vdn")
                    p.op("vector", lambda e: e.tensor_scalar_mul(
                        sc("t1"), sc("ssq"), 1.0 / D),
                        waits=[], inc="vdn")
                    p.vbub()
                    p.op("vector", lambda e: e.tensor_mul(
                        sc("m2t"), sc("nm"), sc("nm")),
                        waits=[], inc="vdn")
                    p.vbub()
                    p.op("vector", lambda e: e.tensor_sub(
                        sc("var"), sc("t1"), sc("m2t")),
                        waits=[], inc="vdn")
                    p.vbub()
                    p.vbub(inc="vdn")
                    vd = p.w("vdn")
                    p.op("scalar", lambda e: e.activation(
                        sc("sd"), sc("var"), AF.Sqrt, bias=epsln[:]),
                        waits=[vd], inc="adn")
                    ad = p.w("adn")
                    p.vbub(waits=[ad])
                    p.vbub()
                    p.op("vector", lambda e: e.reciprocal(
                        sc("rstd"), sc("sd")), waits=[], inc="vdn")
                    p.vbub()
                    p.op("vector", lambda e: e.tensor_scalar(
                        g2[:], g1[:], sc("nm"), None, ALU.add),
                        waits=[], inc="vdn")
                    p.op("vector", lambda e: e.tensor_scalar(
                        g1[:], g2[:], sc("rstd"), None, ALU.mult),
                        waits=[], inc="vdn")
                    p.dma("gpsimd", "gld",
                          out_h[ti * 128:(ti + 1) * 128, :], g1[:],
                          waits=[p.w("vdn")])
                    p.done("g1", "gld")

            for tg in range(4):
                p.dma("sync", "qld", sxg[tg % 2][:],
                      vecs_h[:, tg * 512:(tg + 1) * 512]
                      .partition_broadcast(128),
                      waits=[p.use(f"sxg{tg % 2}")])
                SXLD = p.w("qld")
                for e8 in range(8):
                    for s in range(8):
                        hsub = e8 * 8 + s
                        wls = []
                        for j in range(2):
                            p.dma("sync", "qld", wub[j][:],
                                  wupT3[:, j * 8:(j + 1) * 8,
                                        hsub * 128:(hsub + 1) * 128],
                                  waits=[p.use(f"wub{j}")])
                            wls.append(p.w("qld"))
                        ubn = f"bank{4 + s % 2}"
                        upb = banks[4 + s % 2]
                        for dc in range(16):
                            bw = [p.use(ubn)] if dc == 0 else []
                            lw = [wls[dc // 8]] if dc % 8 == 0 else []
                            p.op("tensor",
                                 lambda e, w2=wub[dc // 8], d=dc, b=upb,
                                 f=(dc == 0), t0=tg:
                                 e.matmul(
                                     b[:],
                                     w2[:, (d % 8) * 128:(d % 8 + 1) * 128],
                                     xqT[:, d * 2048 + t0 * 512:
                                         d * 2048 + (t0 + 1) * 512],
                                     start=f, stop=False),
                                 waits=lw + bw,
                                 inc="tdn" if dc == 7 else None)
                            if dc == 7:
                                p.done("wub0", "tdn")
                        p.op("tensor",
                             lambda e, b=upb: e.matmul(
                                 b[:], zft[:], gam_rep[:, 0:512],
                                 start=False, stop=True),
                             waits=[], inc="tdn")
                        p.done("wub1", "tdn")
                        um = p.w("tdn")
                        p.op("vector",
                             lambda e, b=upb, t0=tg: e.tensor_mul(
                                 sfu[:], b[:], sxg[t0 % 2][:]),
                             waits=[um, SXLD, p.use("sfu")], inc="vdn")
                        p.done(ubn, "vdn")
                        vd = p.w("vdn")
                        p.op("scalar",
                             lambda e: e.activation(hbuf[:], sfu[:], AF.Silu),
                             waits=[vd], inc=None)
                        p.op("scalar",
                             lambda e: e.copy(p.bubA[:], p.bubB[:]),
                             waits=[], inc="adn")
                        p.done("sfu", "adn")
                        ad = p.w("adn")
                        p.op("vector",
                             lambda e, s2=s: e.tensor_copy(
                                 hfhi[:, s2 * 512:(s2 + 1) * 512], hbuf[:]),
                             waits=[ad, p.use("hfeat")], inc="vdn")
                        p.op("vector",
                             lambda e, s2=s: e.scalar_tensor_tensor(
                                 hflo[:, s2 * 512:(s2 + 1) * 512], hbuf[:],
                                 1.0, hfhi[:, s2 * 512:(s2 + 1) * 512],
                                 ALU.mult, ALU.subtract),
                             waits=[], inc="vdn")
                    HFD = p.w("vdn")
                    for half in range(2):
                        for s in range(8):
                            hsub = e8 * 8 + s
                            p.dma("sync", "qld", wdnb[s % 2][:],
                                  wdnT_h[hsub * 128:(hsub + 1) * 128, :],
                                  waits=[p.use(f"wdnb{s % 2}")])
                            wl = p.w("qld")
                            for t2 in range(2):
                                tt = half * 2 + t2
                                for ct in range(4):
                                    gi = t2 * 4 + ct
                                    bname = (f"bank{gi}" if gi < 6
                                             else f"gps{gi - 6}")
                                    for hilo in range(2):
                                        hf = hfhi if hilo == 0 else hflo
                                        st = s == 0 and hilo == 0
                                        bw = [p.use(bname)] if st else []
                                        p.op("tensor",
                                             lambda e, h3=hf, s2=s, t3=tt,
                                             c=ct, g=gi, w2=wdnb[s % 2],
                                             f=st: e.matmul(
                                                 gbank(g),
                                                 h3[:, s2 * 512 + t3 * 128:
                                                    s2 * 512 + (t3 + 1) * 128],
                                                 w2[:, c * 512:(c + 1) * 512],
                                                 start=f, stop=False),
                                             waits=[HFD, wl] + bw,
                                             inc="tdn"
                                             if (hilo == 1 and ct == 3 and
                                                 t2 == 1) else None)
                            p.done(f"wdnb{s % 2}", "tdn")
                        for gi in range(8):
                            p.op("tensor",
                                 lambda e, g=gi: e.matmul(
                                     gbank(g), zbt[:], xqT[:, 0:512],
                                     start=False, stop=True),
                                 waits=[], inc="tdn" if gi == 7 else None)
                        dmm = p.w("tdn")
                        for t2 in range(2):
                            tt = half * 2 + t2
                            for ct in range(4):
                                gi = t2 * 4 + ct
                                bname = (f"bank{gi}" if gi < 6
                                         else f"gps{gi - 6}")
                                col = tt * 2048 + ct * 512
                                oc = oacc2[tg % 2]
                                if e8 == 0:
                                    p.op("vector",
                                         lambda e, g=gi, c2=col, o=oc:
                                         e.tensor_copy(
                                             o[:, c2:c2 + 512], gbank(g)),
                                         waits=[dmm], inc="vdn")
                                else:
                                    p.op("vector",
                                         lambda e, g=gi, c2=col, o=oc:
                                         e.scalar_tensor_tensor(
                                             o[:, c2:c2 + 512], gbank(g),
                                             1.0, o[:, c2:c2 + 512],
                                             ALU.mult, ALU.add),
                                         waits=[dmm], inc="vdn")
                                p.done(bname, "vdn")
                    p.done("hfeat", "tdn")
                p.done(f"sxg{tg % 2}", "vdn")
                if tg >= 1:
                    emit_epilogue(tg - 1)
            emit_epilogue(3)
            p.fence()

        with nc.Block() as block:
            p.emit(block, sems)

    return nc


_NC_CACHE = [None]


def kernel(L0_attn_x, L0_attn_Wq, L0_attn_Wk, L0_attn_Wv, L0_attn_Wo,
           L0_ffn_weight_up, L0_ffn_weight_down, L0_ffn_norm_weight,
           L0_ffn_gamma):
    x = np.ascontiguousarray(np.asarray(L0_attn_x, dtype=np.float32))
    wq = np.ascontiguousarray(np.asarray(L0_attn_Wq, dtype=np.float32))
    wk = np.ascontiguousarray(np.asarray(L0_attn_Wk, dtype=np.float32))
    wv = np.ascontiguousarray(np.asarray(L0_attn_Wv, dtype=np.float32))
    wo = np.ascontiguousarray(np.asarray(L0_attn_Wo, dtype=np.float32))
    wup = np.ascontiguousarray(np.asarray(L0_ffn_weight_up, dtype=np.float32))
    wdn = np.ascontiguousarray(np.asarray(L0_ffn_weight_down, dtype=np.float32))
    nw = np.ascontiguousarray(
        np.asarray(L0_ffn_norm_weight, dtype=np.float32).reshape(1, D))
    gam = np.ascontiguousarray(
        np.asarray(L0_ffn_gamma, dtype=np.float32).reshape(1, D))

    if _NC_CACHE[0] is None:
        _NC_CACHE[0] = build()
    nc = _NC_CACHE[0]

    in_maps = []
    for c in range(8):
        in_maps.append({
            "x": np.ascontiguousarray(x[c]),
            "wq": wq, "wk": wk, "wv": wv, "wo": wo,
            "wup": wup, "wdn": wdn, "nw": nw, "gam": gam,
        })
    res = run_bass_kernel_spmd(nc, in_maps, core_ids=list(range(8)))
    return np.stack([r["out"] for r in res.results], axis=0)


if __name__ == "__main__":
    nc = build()
    print("build ok")
